# version 01
# baseline (speedup 1.0000x reference)
import numpy as np
from contextlib import ExitStack

import concourse.bass as bass
import concourse.mybir as mybir
from concourse.bass_utils import run_bass_kernel_spmd
from concourse.bass_core_rust import IndirectOffsetOnAxis
from concourse.library_config import mlp as LIB_MLP

F32 = mybir.dt.float32
BF16 = mybir.dt.bfloat16
I16 = mybir.dt.int16
I32 = mybir.dt.int32
AF = mybir.ActivationFunctionType
AL = mybir.AluOpType
AX = mybir.AxisListType.X

D = 1024
T = 2048
B = 4
DC = 8
FFN = 4096
NE = 8
C_LOC = 2560
NSLOT = NE * C_LOC
NSLOT_PAD = 20608
CAP = 16384
CH1 = 256
CH2 = 512
DUMMY = B * T
P2N = 8
P2PARTS = "gwchys"
ENGS = ("sync", "scalar", "vector", "tensor", "gpsimd")


class Ctr:
    def __init__(self, nc, es, name):
        self.sem = es.enter_context(nc.semaphore(name))
        self.n = 0


class Prog:

    def __init__(self, nc, es):
        self.nc = nc
        self.ctr = {e: Ctr(nc, es, f"c_{e}") for e in ENGS}
        self.steps = []
        self.seen = {e: {e2: 0 for e2 in ENGS} for e in ENGS}

    def add(self, eng, fn, inck=1):
        waits = []
        for e2 in ENGS:
            v = self.ctr[e2].n
            if v > self.seen[eng][e2]:
                waits.append((self.ctr[e2].sem, v))
                self.seen[eng][e2] = v
        self.steps.append((eng, waits, fn, (self.ctr[eng].sem, inck)))
        self.ctr[eng].n += inck

    def dma(self, eng, fn):
        self.add(eng, fn, inck=16)

    def emit(self, block):
        for eng_name in ENGS:
            steps = [s for s in self.steps if s[0] == eng_name]

            def body(eng, steps=steps):
                for (_e, waits, fn, inc) in steps:
                    for (sem, v) in waits:
                        eng.wait_ge(sem, v)
                    ins = fn(eng)
                    if isinstance(ins, (list, tuple)):
                        ins = ins[-1]
                    assert ins is not None
                    ins.then_inc(inc[0], inc[1])
            getattr(block, eng_name)(body)


def build_program(phases=("p0", "p1a", "p1b", "p15", "p2"), debug=()):
    nc = bass.Bass("TRN2", target_bir_lowering=False, debug=False)
    es = ExitStack()

    def din(name, shape, dt=F32):
        return nc.dram_tensor(name, shape, dt, kind="ExternalInput").ap()

    x_in = din("x", [B, T, D])
    Wk_in = din("Wk", [D, D]); Wv_in = din("Wv", [D, D])
    Wr_in = din("Wr", [D, D]); Wo_in = din("Wo", [D, D])
    vecs_in = din("vecs", [6, D])
    wr_in = din("w_router", [D, NE])
    w1_in = din("w1", [NE, D, FFN])
    w2_in = din("w2", [NE, FFN, D])
    bias_in = din("bias", [1, D])
    maskT_in = din("maskT", [32, 5])

    out = nc.dram_tensor("out", [B * T, D], F32, kind="ExternalOutput").ap()

    def dscr(name, shape, dt):
        if name in debug:
            return nc.dram_tensor(name, shape, dt, kind="ExternalOutput").ap()
        return nc.dram_tensor(name, shape, dt).ap()

    rwkv_hi_d = dscr("rwkv_hi", [D, B * T], BF16)
    rwkv_lo_d = dscr("rwkv_lo", [D, B * T], BF16)
    xln2_cat_d = dscr("xln2_cat", [B * T + 1, 2 * D], BF16)
    tokplane_d = dscr("tokplane", [NSLOT_PAD + 1], F32)
    gateplane_d = dscr("gateplane", [NSLOT_PAD + 1], F32)
    cc_in_d = nc.dram_tensor("cc_in", [B, NE], F32).ap()
    cc_out_d = nc.dram_tensor("cc_out", [32, NE], F32).ap()
    cc_dbg_d = (nc.dram_tensor("cc_dbg", [32, NE], F32, kind="ExternalOutput").ap()
                if "cc_out" in debug else None)

    def sbt(stack, name, shape, dt):
        return stack.enter_context(nc.sbuf_tensor(name, shape, dt))

    ident_f = sbt(es, "ident_f", [128, 128], F32)
    ones128 = sbt(es, "ones128", [128, 128], F32)
    Lstr_f = sbt(es, "Lstr_f", [128, 128], F32)
    iota8 = sbt(es, "iota8", [128, NE], F32)
    iotap = sbt(es, "iotap", [128, 1], F32)
    vecs = sbt(es, "vecs_sb", [128, 6, DC], F32)
    omt = sbt(es, "omt", [128, 3, DC], F32)
    negw = sbt(es, "negw", [128, DC], F32)
    wrh = sbt(es, "wrh", [128, DC, NE], BF16)
    wrl = sbt(es, "wrl", [128, DC, NE], BF16)
    cs_bc = sbt(es, "cs_bc", [128, NE], F32)
    bias_bc = sbt(es, "bias_bc", [128, D], F32)
    stat = sbt(es, "stat", [128, 16], F32)
    wrk = sbt(es, "wrk", [128, 32], F32)
    wrk_i = sbt(es, "wrk_i", [128, 4], I32)
    meta_sb = sbt(es, "meta_sb", [128, 64, 6], F32)
    carry = sbt(es, "carry", [128, NE], F32)
    idx16 = sbt(es, "idx16", [128, 1280], I16)
    tau_bc = sbt(es, "tau_bc", [128, NE], F32)
    lcb_bc = sbt(es, "lcb_bc", [128, B, NE], F32)

    psum = [es.enter_context(nc.psum_tensor(f"ps{i}", [128, 512], F32)) for i in range(8)]

    P = Prog(nc, es)

    es_a = ExitStack()
    wk_hi = sbt(es_a, "wk_hi", [128, DC * D], BF16); wk_lo = sbt(es_a, "wk_lo", [128, DC * D], BF16)
    wv_hi = sbt(es_a, "wv_hi", [128, DC * D], BF16); wv_lo = sbt(es_a, "wv_lo", [128, DC * D], BF16)
    wq_hi = sbt(es_a, "wq_hi", [128, DC * D], BF16); wq_lo = sbt(es_a, "wq_lo", [128, DC * D], BF16)
    x_tok = sbt(es_a, "x_tok", [128, 2 * D], F32)
    sq_scr = sbt(es_a, "sq_scr", [128, D], BF16)
    xln_fm = sbt(es_a, "xln_fm", [128, DC * (CH1 + 1)], F32)
    mix_f = sbt(es_a, "mix_f", [128, DC * CH1], F32)
    mixh = sbt(es_a, "mixh", [128, DC * CH1], BF16)
    mixl = sbt(es_a, "mixl", [128, DC * CH1], BF16)
    k_sb = sbt(es_a, "k_sb", [128, DC * CH1], F32)
    v_sb = sbt(es_a, "v_sb", [128, DC * CH1], F32)
    r_sb = sbt(es_a, "r_sb", [128, DC * CH1], F32)
    e1_sb = sbt(es_a, "e1_sb", [128, DC * CH1], F32)
    e2_sb = sbt(es_a, "e2_sb", [128, DC * CH1], F32)
    a_sb = sbt(es_a, "a_sb", [128, DC * CH1], F32)
    b_sb = sbt(es_a, "b_sb", [128, DC * CH1], F32)
    st_a = sbt(es_a, "st_a", [128, DC], F32)
    st_b = sbt(es_a, "st_b", [128, DC], F32)
    mixf_b = mix_f[:, :].bitcast(BF16)
    rwh_sb = mixf_b[:, 0:DC * CH1]
    rwl_sb = mixf_b[:, DC * CH1:2 * DC * CH1]

    def p0_consts(eng):
        eng.memset(ident_f[:, :], 1.0)
        eng.memset(ones128[:, :], 1.0)
        eng.memset(Lstr_f[:, :], 1.0)
        eng.drain()
        eng.affine_select(ident_f[:, :], ident_f[:, :], pattern=[[-1, 128]],
                          compare_op=AL.is_equal, fill=0.0, base=0, channel_multiplier=1)
        eng.affine_select(Lstr_f[:, :], Lstr_f[:, :], pattern=[[1, 128]],
                          compare_op=AL.is_gt, fill=0.0, base=0, channel_multiplier=-1)
        eng.iota(iota8[:, :], pattern=[[1, NE]], base=0, channel_multiplier=0,
                 allow_small_or_imprecise_dtypes=True)
        return eng.iota(iotap[:, :], pattern=[[1, 1]], base=0, channel_multiplier=1,
                        allow_small_or_imprecise_dtypes=True)
    P.add("gpsimd", p0_consts)

    P.dma("sync", lambda e: e.dma_start(
        out=x_tok[0:48, 0:128],
        in_=vecs_in.rearrange("a (c p) -> (a c) p", p=128)))
    P.add("tensor", lambda e: e.transpose(psum[0][:, 0:48], x_tok[0:48, 0:128], ident_f[0:48, 0:48]))

    def p0_vecs(eng):
        eng.tensor_copy(out=vecs[:, :, :].rearrange("p j c -> p (j c)"), in_=psum[0][:, 0:48])
        eng.drain()
        for j in range(3):
            eng.tensor_scalar(omt[:, j, :], vecs[:, 2 + j, :], -1.0, 1.0,
                              op0=AL.mult, op1=AL.add)
        eng.tensor_scalar_mul(negw[:, :], vecs[:, 1, :], -1.0)
        eng.drain()
        return eng.engine_nop()
    P.add("vector", p0_vecs)

    P.dma("sync", lambda e: e.dma_start(
        out=mix_f[:, 0:DC * NE].rearrange("p (c e) -> p c e", c=DC),
        in_=wr_in.rearrange("(c p) e -> p c e", p=128)))

    def p0_wr(eng):
        eng.tensor_copy(out=wrh[:, :, :].rearrange("p c e -> p (c e)"), in_=mix_f[:, 0:DC * NE])
        eng.drain()
        eng.tensor_tensor(out=wrl[:, :, :].rearrange("p c e -> p (c e)"),
                          in0=mix_f[:, 0:DC * NE],
                          in1=wrh[:, :, :].rearrange("p c e -> p (c e)"), op=AL.subtract)
        eng.drain()
        return eng.engine_nop()
    P.add("vector", p0_wr)

    def p0_cs(eng):
        ins = None
        for dcq in range(DC):
            ins = eng.matmul(psum[1][0:1, 0:NE], ones128[:, 0:1],
                             mix_f[:, dcq * NE:(dcq + 1) * NE],
                             start=(dcq == 0), stop=(dcq == DC - 1))
        return ins
    P.add("tensor", p0_cs)
    P.add("vector", lambda e: e.tensor_copy(out=stat[0:1, 0:NE], in_=psum[1][0:1, 0:NE]))
    P.add("tensor", lambda e: e.matmul(psum[2][:, 0:NE], ones128[0:1, :], stat[0:1, 0:NE],
                                       start=True, stop=True))
    P.add("vector", lambda e: e.tensor_copy(out=cs_bc[:, :], in_=psum[2][:, 0:NE]))

    P.dma("sync", lambda e: e.dma_start(out=x_tok[0:1, 0:D], in_=bias_in[:, :]))

    def p0_bias(eng):
        eng.matmul(psum[1][:, :], ones128[0:1, :], x_tok[0:1, 0:512], start=True, stop=True)
        return eng.matmul(psum[2][:, :], ones128[0:1, :], x_tok[0:1, 512:1024],
                          start=True, stop=True)
    P.add("tensor", p0_bias)

    def p0_bias_ev(eng):
        eng.tensor_copy(out=bias_bc[:, 0:512], in_=psum[1][:, :])
        eng.tensor_copy(out=bias_bc[:, 512:1024], in_=psum[2][:, :])
        eng.drain()
        return eng.engine_nop()
    P.add("vector", p0_bias_ev)

    def prep_attnw(W_in, w_hi, w_lo, stg, scr):
        for rc in range(DC):
            P.dma("sync", lambda e, rc=rc: e.dma_start(out=stg[:, 0:D],
                                                       in_=W_in[rc * 128:(rc + 1) * 128, :]))
            for half in range(2):
                def tr(eng, half=half):
                    ins = None
                    for j in range(4):
                        ccd = half * 4 + j
                        ins = eng.transpose(psum[4 + j][:, 0:128],
                                            stg[:, ccd * 128:(ccd + 1) * 128], ident_f[:, :])
                    return ins
                P.add("tensor", tr)

                def ev(eng, rc=rc, half=half, w_hi=w_hi, w_lo=w_lo):
                    for j in range(4):
                        eng.tensor_copy(out=scr[:, j * 128:(j + 1) * 128], in_=psum[4 + j][:, 0:128])
                    eng.drain()
                    for j in range(4):
                        o = (half * 4 + j) * D + rc * 128
                        eng.tensor_copy(out=w_hi[:, o:o + 128], in_=scr[:, j * 128:(j + 1) * 128])
                    eng.drain()
                    for j in range(4):
                        o = (half * 4 + j) * D + rc * 128
                        eng.tensor_tensor(out=w_lo[:, o:o + 128],
                                          in0=scr[:, j * 128:(j + 1) * 128],
                                          in1=w_hi[:, o:o + 128], op=AL.subtract)
                    eng.drain()
                    return eng.engine_nop()
                P.add("vector", ev)

    if "p1a" in phases:
        prep_attnw(Wk_in, wk_hi, wk_lo, k_sb, v_sb)
        prep_attnw(Wv_in, wv_hi, wv_lo, k_sb, v_sb)
        prep_attnw(Wr_in, wq_hi, wq_lo, k_sb, v_sb)

    def p1a_chunk(b, ci):
        t0 = ci * CH1
        first = (ci == 0)
        P.dma("sync", lambda e: e.dma_start(out=x_tok[:, 0:D], in_=x_in[b, t0:t0 + 128, :]))
        P.dma("sync", lambda e: e.dma_start(out=x_tok[:, D:2 * D],
                                            in_=x_in[b, t0 + 128:t0 + 256, :]))

        def ln_sums(eng):
            for ti in range(2):
                eng.reduce_sum(stat[:, ti * 8:ti * 8 + 1], x_tok[:, ti * D:(ti + 1) * D], axis=AX)
            eng.drain()
            return eng.engine_nop()
        P.add("vector", ln_sums)

        def ln_sq(eng):
            ins = None
            for ti in range(2):
                if ti:
                    eng.drain()
                ins = eng.activation(out=sq_scr[:, :], in_=x_tok[:, ti * D:(ti + 1) * D],
                                     func=AF.Square, accum_out=stat[:, ti * 8 + 2:ti * 8 + 3])
            return ins
        P.add("scalar", ln_sq)

        def ln_var(eng):
            for ti in range(2):
                s = stat[:, ti * 8:(ti + 1) * 8]
                eng.tensor_scalar_mul(s[:, 1:2], s[:, 0:1], 1.0 / D)
                eng.tensor_scalar_mul(s[:, 3:4], s[:, 2:3], 1.0 / D)
            eng.drain()
            for ti in range(2):
                s = stat[:, ti * 8:(ti + 1) * 8]
                eng.tensor_tensor(s[:, 4:5], s[:, 1:2], s[:, 1:2], op=AL.mult)
            eng.drain()
            for ti in range(2):
                s = stat[:, ti * 8:(ti + 1) * 8]
                eng.tensor_tensor(s[:, 5:6], s[:, 3:4], s[:, 4:5], op=AL.subtract)
            eng.drain()
            for ti in range(2):
                s = stat[:, ti * 8:(ti + 1) * 8]
                eng.tensor_scalar_add(s[:, 5:6], s[:, 5:6], 1e-5)
            eng.drain()
            return eng.engine_nop()
        P.add("vector", ln_var)

        def ln_sqrt(eng):
            ins = None
            for ti in range(2):
                s = stat[:, ti * 8:(ti + 1) * 8]
                ins = eng.activation(out=s[:, 6:7], in_=s[:, 5:6], func=AF.Sqrt)
            return ins
        P.add("scalar", ln_sqrt)

        def ln_norm(eng):
            for ti in range(2):
                s = stat[:, ti * 8:(ti + 1) * 8]
                eng.reciprocal(s[:, 7:8], s[:, 6:7])
            eng.drain()
            for ti in range(2):
                s = stat[:, ti * 8:(ti + 1) * 8]
                eng.tensor_scalar(x_tok[:, ti * D:(ti + 1) * D], x_tok[:, ti * D:(ti + 1) * D],
                                  s[:, 1:2], s[:, 7:8], op0=AL.subtract, op1=AL.mult)
            eng.drain()
            return eng.engine_nop()
        P.add("vector", ln_norm)

        def shift_prep(eng):
            for dc in range(DC):
                o = dc * (CH1 + 1)
                if first:
                    eng.memset(xln_fm[:, o:o + 1], 0.0)
                else:
                    eng.tensor_copy(out=xln_fm[:, o:o + 1], in_=xln_fm[:, o + CH1:o + CH1 + 1])
            eng.drain()
            return eng.engine_nop()
        P.add("vector", shift_prep)

        for ti in range(2):
            for g in range(2):
                def tr(eng, ti=ti, g=g):
                    ins = None
                    for j in range(4):
                        dc = g * 4 + j
                        ins = eng.transpose(psum[4 + j][:, 0:128],
                                            x_tok[:, ti * D + dc * 128:ti * D + (dc + 1) * 128],
                                            ident_f[:, :])
                    return ins
                P.add("tensor", tr)

                def ev(eng, ti=ti, g=g):
                    for j in range(4):
                        dc = g * 4 + j
                        o = dc * (CH1 + 1) + 1 + ti * 128
                        eng.tensor_copy(out=xln_fm[:, o:o + 128], in_=psum[4 + j][:, 0:128])
                    eng.drain()
                    return eng.engine_nop()
                P.add("vector", ev)

        projs = ((k_sb, wk_hi, wk_lo, 2), (v_sb, wv_hi, wv_lo, 3), (r_sb, wq_hi, wq_lo, 4))
        for dst, w_hi, w_lo, vj in projs:
            def mix(eng, vj=vj):
                for dc in range(DC):
                    o = dc * (CH1 + 1)
                    eng.tensor_scalar_mul(mix_f[:, dc * CH1:(dc + 1) * CH1],
                                          xln_fm[:, o:o + CH1], omt[:, vj - 2, dc:dc + 1])
                eng.drain()
                for dc in range(DC):
                    o = dc * (CH1 + 1)
                    eng.scalar_tensor_tensor(mix_f[:, dc * CH1:(dc + 1) * CH1],
                                             xln_fm[:, o + 1:o + 1 + CH1],
                                             vecs[:, vj, dc:dc + 1],
                                             mix_f[:, dc * CH1:(dc + 1) * CH1],
                                             op0=AL.mult, op1=AL.add)
                eng.drain()
                return eng.engine_nop()
            P.add("vector", mix)
            if vj == 4:
                P.add("scalar", lambda e: e.activation(out=mix_f[:, :], in_=mix_f[:, :],
                                                       func=AF.Sigmoid))
            P.add("scalar", lambda e: e.copy(out=mixh[:, :], in_=mix_f[:, :]))

            def lo(eng):
                eng.tensor_tensor(out=mixl[:, :], in0=mix_f[:, :], in1=mixh[:, :], op=AL.subtract)
                eng.drain()
                return eng.engine_nop()
            P.add("vector", lo)

            for ec in range(DC):
                def pj(eng, ec=ec, w_hi=w_hi, w_lo=w_lo):
                    bank = psum[ec % 2]
                    ins = None
                    for dc in range(DC):
                        lh = w_hi[:, dc * D + ec * 128:dc * D + ec * 128 + 128]
                        ll = w_lo[:, dc * D + ec * 128:dc * D + ec * 128 + 128]
                        rh = mixh[:, dc * CH1:(dc + 1) * CH1]
                        rl = mixl[:, dc * CH1:(dc + 1) * CH1]
                        eng.matmul(bank[:, 0:CH1], lh, rh, start=(dc == 0), stop=False)
                        eng.matmul(bank[:, 0:CH1], lh, rl, start=False, stop=False)
                        ins = eng.matmul(bank[:, 0:CH1], ll, rh, start=False, stop=(dc == DC - 1))
                    return ins
                P.add("tensor", pj)
                P.add("vector", lambda e, ec=ec, dst=dst: e.tensor_copy(
                    out=dst[:, ec * CH1:(ec + 1) * CH1], in_=psum[ec % 2][:, 0:CH1]))

        def sprep(eng):
            for dc in range(DC):
                sl = k_sb[:, dc * CH1:(dc + 1) * CH1]
                eng.tensor_scalar_add(sl, sl, vecs[:, 0, dc:dc + 1])
            eng.drain()
            for dc in range(DC):
                sl = k_sb[:, dc * CH1:(dc + 1) * CH1]
                eng.tensor_scalar(e2_sb[:, dc * CH1:(dc + 1) * CH1], sl,
                                  vecs[:, 1, dc:dc + 1], 0.0, op0=AL.subtract, op1=AL.min)
                eng.tensor_single_scalar(e1_sb[:, dc * CH1:(dc + 1) * CH1], sl,
                                         vecs[:, 1, dc:dc + 1], op=AL.max)
            eng.drain()
            return eng.engine_nop()
        P.add("vector", sprep)

        def sexp(eng):
            for dc in range(DC):
                eng.activation(out=e1_sb[:, dc * CH1:(dc + 1) * CH1],
                               in_=e1_sb[:, dc * CH1:(dc + 1) * CH1], func=AF.Exp,
                               bias=negw[:, dc:dc + 1], scale=-1.0)
            ins = None
            for dc in range(DC):
                ins = eng.activation(out=e2_sb[:, dc * CH1:(dc + 1) * CH1],
                                     in_=e2_sb[:, dc * CH1:(dc + 1) * CH1], func=AF.Exp)
            return ins
        P.add("scalar", sexp)

        def scans(eng):
            eng.tensor_tensor(out=v_sb[:, :], in0=e2_sb[:, :], in1=v_sb[:, :], op=AL.mult)
            if first:
                eng.memset(st_a[:, :], 0.0)
                eng.memset(st_b[:, :], 0.0)
            eng.drain()
            for dc in range(DC):
                eng.tensor_tensor_scan(a_sb[:, dc * CH1:(dc + 1) * CH1],
                                       e1_sb[:, dc * CH1:(dc + 1) * CH1],
                                       v_sb[:, dc * CH1:(dc + 1) * CH1],
                                       st_a[:, dc:dc + 1], AL.mult, AL.add)
                eng.tensor_tensor_scan(b_sb[:, dc * CH1:(dc + 1) * CH1],
                                       e1_sb[:, dc * CH1:(dc + 1) * CH1],
                                       e2_sb[:, dc * CH1:(dc + 1) * CH1],
                                       st_b[:, dc:dc + 1], AL.mult, AL.add)
            eng.drain()
            eng.tensor_copy(out=st_a[:, :],
                            in_=a_sb[:, :].rearrange("p (c t) -> p c t", c=DC)[:, :, CH1 - 1])
            eng.tensor_copy(out=st_b[:, :],
                            in_=b_sb[:, :].rearrange("p (c t) -> p c t", c=DC)[:, :, CH1 - 1])
            eng.drain()
            eng.reciprocal(b_sb[:, :], b_sb[:, :])
            eng.drain()
            eng.tensor_tensor(out=a_sb[:, :], in0=a_sb[:, :], in1=b_sb[:, :], op=AL.mult)
            eng.drain()
            eng.tensor_tensor(out=a_sb[:, :], in0=a_sb[:, :], in1=r_sb[:, :], op=AL.mult)
            eng.drain()
            return eng.engine_nop()
        P.add("vector", scans)

        P.add("scalar", lambda e: e.copy(out=rwh_sb, in_=a_sb[:, :]))

        def rlo(eng):
            eng.tensor_tensor(out=rwl_sb, in0=a_sb[:, :], in1=rwh_sb, op=AL.subtract)
            eng.drain()
            return eng.engine_nop()
        P.add("vector", rlo)

        if b == 0 and ci == 0 and "dbg1" in debug:
            for nm, buf in (("dbg_xlnfm", xln_fm), ("dbg_k", k_sb), ("dbg_v", v_sb),
                            ("dbg_r", r_sb), ("dbg_e1", e1_sb), ("dbg_e2", e2_sb),
                            ("dbg_a", a_sb)):
                dd = nc.dram_tensor(nm, [128, buf.shape[1] if hasattr(buf, 'shape') else 0],
                                    F32, kind="ExternalOutput").ap()
                P.dma("scalar", lambda e, dd=dd, buf=buf: e.dma_start(out=dd[:, :], in_=buf[:, :]))
        col0 = b * T + t0
        P.dma("scalar", lambda e: e.dma_start(
            out=rwkv_hi_d.rearrange("(c p) t -> p c t", p=128)[:, :, col0:col0 + CH1],
            in_=rwh_sb.rearrange("p (c t) -> p c t", c=DC)))
        P.dma("scalar", lambda e: e.dma_start(
            out=rwkv_lo_d.rearrange("(c p) t -> p c t", p=128)[:, :, col0:col0 + CH1],
            in_=rwl_sb.rearrange("p (c t) -> p c t", c=DC)))

    if "p1a" in phases:
        for b in range(B):
            for ci in range(T // CH1):
                p1a_chunk(b, ci)

    es_a.close()

    es_b = ExitStack()
    wo_hi = sbt(es_b, "wo_hi", [128, DC * D], BF16)
    wo_lo = sbt(es_b, "wo_lo", [128, DC * D], BF16)
    stg_b = sbt(es_b, "stg_b", [128, D], F32)
    scr_b = sbt(es_b, "scr_b", [128, 512], F32)
    rwvh_sb = sbt(es_b, "rwvh_sb", [128, DC, CH2], BF16)
    rwvl_sb = sbt(es_b, "rwvl_sb", [128, DC, CH2], BF16)
    x4_sb = sbt(es_b, "x4_sb", [128, 4 * D], F32)
    ao_fm = sbt(es_b, "ao_fm", [128, DC * CH2], F32)
    xaf_hi = sbt(es_b, "xaf_hi", [128, DC * CH2], BF16)
    xaf_lo = sbt(es_b, "xaf_lo", [128, DC * CH2], BF16)
    xat_sb = sbt(es_b, "xat_sb", [128, 4 * D], F32)
    raw_sb = sbt(es_b, "raw_sb", [NE, CH2], F32)
    lgt_sb = sbt(es_b, "lgt_sb", [128, 4 * NE], F32)
    sc_sb = sbt(es_b, "sc_sb", [128, 4 * NE], F32)
    sq2_scr = sbt(es_b, "sq2_scr", [128, D], BF16)
    xl2 = sbt(es_b, "xl2", [128, D], F32)
    xl2h = sbt(es_b, "xl2h", [128, D], BF16)
    xl2l = sbt(es_b, "xl2l", [128, D], BF16)

    def p1b_chunk(b, ci):
        t0 = ci * CH2
        col0 = b * T + t0
        P.dma("sync", lambda e: e.dma_start(
            out=rwvh_sb[:, :, :],
            in_=rwkv_hi_d.rearrange("(c p) t -> p c t", p=128)[:, :, col0:col0 + CH2]))
        P.dma("sync", lambda e: e.dma_start(
            out=rwvl_sb[:, :, :],
            in_=rwkv_lo_d.rearrange("(c p) t -> p c t", p=128)[:, :, col0:col0 + CH2]))
        for ti in range(4):
            P.dma("sync", lambda e, ti=ti: e.dma_start(
                out=x4_sb[:, ti * D:(ti + 1) * D],
                in_=x_in[b, t0 + ti * 128:t0 + (ti + 1) * 128, :]))

        for oc in range(DC):
            def pj(eng, oc=oc):
                bank = psum[oc % 2]
                ins = None
                for dc in range(DC):
                    lh = wo_hi[:, dc * D + oc * 128:dc * D + oc * 128 + 128]
                    ll = wo_lo[:, dc * D + oc * 128:dc * D + oc * 128 + 128]
                    rh = rwvh_sb[:, dc, :]
                    rl = rwvl_sb[:, dc, :]
                    eng.matmul(bank[:, :], lh, rh, start=(dc == 0), stop=False)
                    eng.matmul(bank[:, :], lh, rl, start=False, stop=False)
                    ins = eng.matmul(bank[:, :], ll, rh, start=False, stop=(dc == DC - 1))
                return ins
            P.add("tensor", pj)
            P.add("scalar", lambda e, oc=oc: e.activation(
                out=ao_fm[:, oc * CH2:(oc + 1) * CH2], in_=psum[oc % 2][:, :],
                func=AF.Identity, bias=vecs[:, 5, oc:oc + 1]))

        for ti in range(4):
            for g in range(2):
                def tr(eng, ti=ti, g=g):
                    ins = None
                    for j in range(4):
                        dc = g * 4 + j
                        ins = eng.transpose(psum[4 + j][:, 0:128],
                                            x4_sb[:, ti * D + dc * 128:ti * D + (dc + 1) * 128],
                                            ident_f[:, :])
                    return ins
                P.add("tensor", tr)

                def ev(eng, ti=ti, g=g):
                    for j in range(4):
                        dc = g * 4 + j
                        o = dc * CH2 + ti * 128
                        eng.scalar_tensor_tensor(ao_fm[:, o:o + 128], psum[4 + j][:, 0:128],
                                                 1.0, ao_fm[:, o:o + 128],
                                                 op0=AL.mult, op1=AL.add)
                    eng.drain()
                    return eng.engine_nop()
                P.add("vector", ev)

        P.add("scalar", lambda e: e.copy(out=xaf_hi[:, :], in_=ao_fm[:, :]))

        def xlo(eng):
            eng.tensor_tensor(out=xaf_lo[:, :], in0=ao_fm[:, :], in1=xaf_hi[:, :], op=AL.subtract)
            eng.drain()
            return eng.engine_nop()
        P.add("vector", xlo)

        def rawmm(eng):
            bank = psum[2]
            ins = None
            for dc in range(DC):
                eng.matmul(bank[0:NE, 0:CH2], wrh[:, dc, :], xaf_hi[:, dc * CH2:(dc + 1) * CH2],
                           start=(dc == 0), stop=False)
                eng.matmul(bank[0:NE, 0:CH2], wrh[:, dc, :], xaf_lo[:, dc * CH2:(dc + 1) * CH2],
                           start=False, stop=False)
                ins = eng.matmul(bank[0:NE, 0:CH2], wrl[:, dc, :],
                                 xaf_hi[:, dc * CH2:(dc + 1) * CH2],
                                 start=False, stop=(dc == DC - 1))
            return ins
        P.add("tensor", rawmm)
        P.add("vector", lambda e: e.tensor_copy(out=raw_sb[:, :], in_=psum[2][0:NE, 0:CH2]))

        for ti in range(4):
            for g in range(2):
                def tr2(eng, ti=ti, g=g):
                    ins = None
                    for j in range(4):
                        dc = g * 4 + j
                        o = dc * CH2 + ti * 128
                        ins = eng.transpose(psum[4 + j][:, 0:128], ao_fm[:, o:o + 128],
                                            ident_f[:, :])
                    return ins
                P.add("tensor", tr2)

                def ev2(eng, ti=ti, g=g):
                    for j in range(4):
                        dc = g * 4 + j
                        eng.tensor_copy(out=xat_sb[:, ti * D + dc * 128:ti * D + (dc + 1) * 128],
                                        in_=psum[4 + j][:, 0:128])
                    eng.drain()
                    return eng.engine_nop()
                P.add("vector", ev2)

        def trraw(eng):
            ins = None
            for ti in range(4):
                ins = eng.transpose(psum[4 + ti][:, 0:NE], raw_sb[:, ti * 128:(ti + 1) * 128],
                                    ident_f[0:NE, 0:NE])
            return ins
        P.add("tensor", trraw)

        def evraw(eng):
            for ti in range(4):
                eng.tensor_copy(out=lgt_sb[:, ti * NE:(ti + 1) * NE], in_=psum[4 + ti][:, 0:NE])
            eng.drain()
            return eng.engine_nop()
        P.add("vector", evraw)

        for ti in range(4):
            trow = col0 + ti * 128
            tile_idx = b * (T // 128) + ci * 4 + ti
            xt = xat_sb[:, ti * D:(ti + 1) * D]

            def ln2a(eng, xt=xt):
                eng.reduce_sum(stat[:, 0:1], xt, axis=AX)
                eng.drain()
                return eng.engine_nop()
            P.add("vector", ln2a)
            P.add("scalar", lambda e, xt=xt: e.activation(out=sq2_scr[:, :], in_=xt,
                                                          func=AF.Square, accum_out=stat[:, 2:3]))

            def ln2b(eng):
                eng.tensor_scalar_mul(stat[:, 1:2], stat[:, 0:1], 1.0 / D)
                eng.tensor_scalar_mul(stat[:, 3:4], stat[:, 2:3], 1.0 / D)
                eng.drain()
                eng.tensor_tensor(stat[:, 4:5], stat[:, 1:2], stat[:, 1:2], op=AL.mult)
                eng.drain()
                eng.tensor_tensor(stat[:, 5:6], stat[:, 3:4], stat[:, 4:5], op=AL.subtract)
                eng.drain()
                eng.tensor_scalar_add(stat[:, 5:6], stat[:, 5:6], 1e-5)
                eng.drain()
                return eng.engine_nop()
            P.add("vector", ln2b)
            P.add("scalar", lambda e: e.activation(out=stat[:, 6:7], in_=stat[:, 5:6], func=AF.Sqrt))

            def ln2c(eng, xt=xt):
                eng.reciprocal(stat[:, 7:8], stat[:, 6:7])
                eng.drain()
                eng.tensor_scalar(xl2[:, :], xt, stat[:, 1:2], stat[:, 7:8],
                                  op0=AL.subtract, op1=AL.mult)
                eng.drain()
                eng.tensor_tensor(out=xt, in0=xt, in1=bias_bc[:, :], op=AL.add)
                eng.drain()
                return eng.engine_nop()
            P.add("vector", ln2c)
            P.add("scalar", lambda e: e.copy(out=xl2h[:, :], in_=xl2[:, :]))

            def xl2lo(eng):
                eng.tensor_tensor(out=xl2l[:, :], in0=xl2[:, :], in1=xl2h[:, :], op=AL.subtract)
                eng.drain()
                return eng.engine_nop()
            P.add("vector", xl2lo)
            P.dma("scalar", lambda e, trow=trow: e.dma_start(
                out=xln2_cat_d[trow:trow + 128, 0:D], in_=xl2h[:, :]))
            P.dma("scalar", lambda e, trow=trow: e.dma_start(
                out=xln2_cat_d[trow:trow + 128, D:2 * D], in_=xl2l[:, :]))
            P.dma("scalar", lambda e, trow=trow, xt=xt: e.dma_start(out=out[trow:trow + 128, :],
                                                                    in_=xt))

            def smax(eng, ti=ti):
                sc = sc_sb[:, ti * NE:(ti + 1) * NE]
                eng.scalar_tensor_tensor(sc, cs_bc[:, :], stat[:, 1:2],
                                         lgt_sb[:, ti * NE:(ti + 1) * NE],
                                         op0=AL.mult, op1=AL.subtract)
                eng.tensor_scalar_mul(stat[:, 8:9], stat[:, 7:8], -1.0)
                eng.drain()
                eng.tensor_scalar_mul(sc, sc, stat[:, 8:9])
                eng.drain()
                eng.reduce_max(wrk[:, 0:1], sc, axis=AX)
                eng.drain()
                eng.tensor_scalar_mul(wrk[:, 1:2], wrk[:, 0:1], -1.0)
                eng.drain()
                return eng.engine_nop()
            P.add("vector", smax)
            P.add("scalar", lambda e, ti=ti: e.activation(
                out=sc_sb[:, ti * NE:(ti + 1) * NE], in_=sc_sb[:, ti * NE:(ti + 1) * NE],
                func=AF.Exp, bias=wrk[:, 1:2]))

            def smax2(eng, ti=ti, tile_idx=tile_idx):
                sc = sc_sb[:, ti * NE:(ti + 1) * NE]
                m = meta_sb[:, tile_idx, :]
                eng.reduce_sum(wrk[:, 2:3], sc, axis=AX)
                eng.drain()
                eng.reciprocal(wrk[:, 3:4], wrk[:, 2:3])
                eng.drain()
                eng.tensor_scalar_mul(sc, sc, wrk[:, 3:4])
                eng.drain()
                eng.reduce_max(m[:, 0:1], sc, axis=AX)
                eng.drain()
                eng.tensor_scalar(wrk[:, 4:12], sc, m[:, 0:1], None, op0=AL.is_equal)
                eng.drain()
                eng.scalar_tensor_tensor(wrk[:, 12:20], wrk[:, 4:12], -1e30, sc,
                                         op0=AL.mult, op1=AL.add)
                eng.drain()
                eng.reduce_max(m[:, 1:2], wrk[:, 12:20], axis=AX)
                eng.drain()
                eng.tensor_scalar(wrk[:, 12:20], wrk[:, 12:20], m[:, 1:2], None, op0=AL.is_equal)
                eng.drain()
                eng.tensor_tensor(wrk[:, 20:28], wrk[:, 4:12], iota8[:, :], op=AL.mult)
                eng.drain()
                eng.reduce_sum(m[:, 2:3], wrk[:, 20:28], axis=AX)
                eng.drain()
                eng.tensor_tensor(wrk[:, 20:28], wrk[:, 12:20], iota8[:, :], op=AL.mult)
                eng.drain()
                eng.reduce_sum(m[:, 3:4], wrk[:, 20:28], axis=AX)
                eng.drain()
                eng.tensor_tensor(wrk[:, 4:12], wrk[:, 4:12], wrk[:, 12:20], op=AL.add)
                eng.drain()
                return eng.engine_nop()
            P.add("vector", smax2)

            if ci == 0 and ti == 0:
                def czero(eng):
                    eng.memset(carry[:, :], 0.0)
                    eng.drain()
                    return eng.engine_nop()
                P.add("vector", czero)

            def pref(eng):
                eng.matmul(psum[3][:, 0:NE], Lstr_f[:, :], wrk[:, 4:12], start=True, stop=False)
                eng.matmul(psum[3][:, 0:NE], ones128[0:1, :], carry[0:1, :],
                           start=False, stop=True)
                eng.matmul(psum[2][0:1, 0:NE], ones128[:, 0:1], wrk[:, 4:12],
                           start=True, stop=False)
                return eng.matmul(psum[2][0:1, 0:NE], ones128[0:1, 0:1], carry[0:1, :],
                                  start=False, stop=True)
            P.add("tensor", pref)

            def pref_ev(eng, tile_idx=tile_idx):
                m = meta_sb[:, tile_idx, :]
                eng.tensor_copy(out=wrk[:, 12:20], in_=psum[3][:, 0:NE])
                eng.tensor_copy(out=carry[0:1, :], in_=psum[2][0:1, 0:NE])
                eng.drain()
                eng.tensor_scalar(wrk[:, 20:28], iota8[:, :], m[:, 2:3], None, op0=AL.is_equal)
                eng.drain()
                eng.tensor_tensor(wrk[:, 20:28], wrk[:, 20:28], wrk[:, 12:20], op=AL.mult)
                eng.drain()
                eng.reduce_sum(m[:, 4:5], wrk[:, 20:28], axis=AX)
                eng.drain()
                eng.tensor_scalar(wrk[:, 20:28], iota8[:, :], m[:, 3:4], None, op0=AL.is_equal)
                eng.drain()
                eng.tensor_tensor(wrk[:, 20:28], wrk[:, 20:28], wrk[:, 12:20], op=AL.mult)
                eng.drain()
                eng.reduce_sum(m[:, 5:6], wrk[:, 20:28], axis=AX)
                eng.drain()
                return eng.engine_nop()
            P.add("vector", pref_ev)


        if ci == T // CH2 - 1:
            P.dma("scalar", lambda e, b=b: e.dma_start(out=cc_in_d[b:b + 1, :], in_=carry[0:1, :]))

    if "p1b" in phases:
        prep_attnw(Wo_in, wo_hi, wo_lo, x4_sb, scr_b)
        for b in range(B):
            for ci in range(T // CH2):
                p1b_chunk(b, ci)

        def zrow(eng):
            eng.memset(xl2h[:, :], 0.0)
            eng.memset(xl2l[:, :], 0.0)
            eng.drain()
            return eng.engine_nop()
        P.add("vector", zrow)
        P.dma("scalar", lambda e: e.dma_start(out=xln2_cat_d[DUMMY:DUMMY + 1, 0:D], in_=xl2h[0:1, :]))
        P.dma("scalar", lambda e: e.dma_start(out=xln2_cat_d[DUMMY:DUMMY + 1, D:2 * D], in_=xl2l[0:1, :]))

    def p15():
        def tinit(eng):
            eng.memset(x4_sb[:, 0:161], 0.0)
            eng.drain()
            return eng.engine_nop()
        P.add("vector", tinit)
        P.dma("scalar", lambda e: e.dma_start(
            out=tokplane_d[0:NSLOT_PAD].rearrange("(p a) -> p a", p=128),
            in_=x4_sb[:, 0:161]))
        P.dma("scalar", lambda e: e.dma_start(
            out=gateplane_d[0:NSLOT_PAD].rearrange("(p a) -> p a", p=128),
            in_=x4_sb[:, 0:161]))

        def ag(eng):
            return eng.collective_compute(
                "AllGather", AL.bypass, replica_groups=[list(range(8))],
                ins=[cc_in_d[:, :]], outs=[cc_out_d[:, :]])
        P.add("gpsimd", ag)
        P.dma("sync", lambda e: e.dma_start(out=x4_sb[0:32, 1024:1024 + NE], in_=cc_out_d[:, :]))
        if cc_dbg_d is not None:
            P.dma("gpsimd", lambda e: e.dma_start(out=cc_dbg_d[:, :], in_=x4_sb[0:32, 1024:1024 + NE]))
        P.dma("sync", lambda e: e.dma_start(out=x4_sb[0:32, 1040:1045], in_=maskT_in[:, :]))

        P.add("tensor", lambda e: e.matmul(psum[3][0:5, 0:NE], x4_sb[0:32, 1040:1045],
                                           x4_sb[0:32, 1024:1024 + NE], start=True, stop=True))

        def basev(eng):
            eng.tensor_copy(out=x4_sb[0:5, 2048:2048 + NE], in_=psum[3][0:5, 0:NE])
            eng.drain()
            return eng.engine_nop()
        P.add("vector", basev)
        P.dma("gpsimd", lambda e: e.dma_start(out=x4_sb[0:1, 2560:2560 + 40],
                                              in_=x4_sb[0:5, 2048:2048 + NE]))

        for b_ in range(B):
            P.add("tensor", lambda e, b_=b_: e.matmul(
                psum[3][:, 0:NE], ones128[0:1, :], x4_sb[0:1, 2560 + b_ * NE:2560 + (b_ + 1) * NE],
                start=True, stop=True))
            P.add("vector", lambda e, b_=b_: e.tensor_copy(out=lcb_bc[:, b_, :],
                                                           in_=psum[3][:, 0:NE]))
        P.add("tensor", lambda e: e.matmul(psum[3][:, 0:NE], ones128[0:1, :],
                                           x4_sb[0:1, 2592:2600], start=True, stop=True))

        def crossv(eng):
            eng.tensor_copy(out=wrk[:, 0:8], in_=psum[3][:, 0:NE])
            eng.drain()
            for b_ in range(B):
                eng.tensor_tensor(lcb_bc[:, b_, :], lcb_bc[:, b_, :], wrk[:, 0:8],
                                  op=AL.subtract)
            eng.tensor_scalar(tau_bc[:, :], wrk[:, 0:8], -1.0, float(CAP),
                              op0=AL.mult, op1=AL.add)
            eng.drain()
            return eng.engine_nop()
        P.add("vector", crossv)

        for tile_idx in range(64):
            b_ = tile_idx // 16
            trow = tile_idx * 128

            def slots(eng, tile_idx=tile_idx, b_=b_, trow=trow):
                m = meta_sb[:, tile_idx, :]
                for ch, (ecol, rcol) in enumerate(((2, 4), (3, 5))):
                    w0 = 4 + ch * 12
                    oh = wrk[:, w0:w0 + 8]
                    eng.tensor_scalar(oh, iota8[:, :], m[:, ecol:ecol + 1], None, op0=AL.is_equal)
                    eng.drain()
                    eng.tensor_tensor(oh, oh, lcb_bc[:, b_, :], op=AL.mult)
                    eng.drain()
                    eng.reduce_sum(wrk[:, w0 + 8:w0 + 9], oh, axis=AX)
                    eng.drain()
                    eng.tensor_tensor(wrk[:, w0 + 8:w0 + 9], wrk[:, w0 + 8:w0 + 9],
                                      m[:, rcol:rcol + 1], op=AL.add)
                    eng.drain()
                    eng.tensor_scalar(wrk[:, w0 + 9:w0 + 10], m[:, ecol:ecol + 1],
                                      float(C_LOC), None, op0=AL.mult)
                    eng.drain()
                    eng.tensor_tensor(wrk[:, w0 + 9:w0 + 10], wrk[:, w0 + 9:w0 + 10],
                                      wrk[:, w0 + 8:w0 + 9], op=AL.add)
                    eng.drain()
                    eng.tensor_scalar(wrk[:, w0 + 10:w0 + 11], wrk[:, w0 + 8:w0 + 9],
                                      float(C_LOC), None, op0=AL.is_ge)
                    eng.drain()
                    eng.tensor_scalar(wrk[:, w0 + 11:w0 + 12], wrk[:, w0 + 9:w0 + 10],
                                      -1.0, float(NSLOT_PAD), op0=AL.mult, op1=AL.add)
                    eng.drain()
                    eng.tensor_tensor(wrk[:, w0 + 11:w0 + 12], wrk[:, w0 + 11:w0 + 12],
                                      wrk[:, w0 + 10:w0 + 11], op=AL.mult)
                    eng.drain()
                    eng.tensor_tensor(wrk[:, w0 + 9:w0 + 10], wrk[:, w0 + 9:w0 + 10],
                                      wrk[:, w0 + 11:w0 + 12], op=AL.add)
                    eng.drain()
                eng.tensor_copy(out=wrk_i[:, 0:1], in_=wrk[:, 13:14])
                eng.tensor_copy(out=wrk_i[:, 1:2], in_=wrk[:, 25:26])
                eng.tensor_scalar_add(wrk[:, 0:1], iotap[:, :], float(trow))
                eng.drain()
                return eng.engine_nop()
            P.add("vector", slots)
            for chn in range(2):
                def scat(eng, chn=chn):
                    return eng.indirect_dma_start(
                        out=tokplane_d.rearrange("(a o) -> a o", o=1),
                        out_offset=IndirectOffsetOnAxis(wrk_i[:, chn:chn + 1], 0),
                        in_=wrk[:, 0:1], in_offset=None)
                P.dma("gpsimd", scat)

                def scatg(eng, chn=chn, tile_idx=tile_idx):
                    return eng.indirect_dma_start(
                        out=gateplane_d.rearrange("(a o) -> a o", o=1),
                        out_offset=IndirectOffsetOnAxis(wrk_i[:, chn:chn + 1], 0),
                        in_=meta_sb[:, tile_idx, chn:chn + 1], in_offset=None)
                P.dma("gpsimd", scatg)

        def idxz(eng):
            eng.memset(idx16[:, :], 0)
            eng.drain()
            return eng.engine_nop()
        P.add("vector", idxz)
        for t_ in range(11):
            r0 = t_ * 128
            rows = min(128, 1280 - r0)
            if rows <= 0:
                continue
            P.dma("sync", lambda e, r0=r0, rows=rows: e.dma_start(
                out=x4_sb[0:rows, 0:16],
                in_=tokplane_d[r0 * 16:(r0 + rows) * 16].rearrange("(a b) -> a b", b=16)))
            P.add("tensor", lambda e, rows=rows: e.transpose(
                psum[3][0:16, 0:rows], x4_sb[0:rows, 0:16], ident_f[0:rows, 0:rows]))

            def evt(eng, r0=r0, rows=rows):
                eng.tensor_copy(out=scr_b[0:16, 0:rows], in_=psum[3][0:16, 0:rows])
                eng.drain()
                eng.tensor_copy(out=idx16[0:16, r0:r0 + rows], in_=scr_b[0:16, 0:rows])
                eng.drain()
                return eng.engine_nop()
            P.add("vector", evt)

        P.dma("gpsimd", lambda e: e.dma_start(out=idx16[16:32, :], in_=idx16[0:16, :]))

    if "p15" in phases:
        p15()

    es_b.close()

    es_c = ExitStack()
    NTK = C_LOC // 256
    FCQ = 8
    w1c_hi = sbt(es_c, "w1c_hi", [128, FCQ * D], BF16)
    w1c_lo = sbt(es_c, "w1c_lo", [128, FCQ * D], BF16)
    w2c_hi = sbt(es_c, "w2c_hi", [128, FCQ * D], BF16)
    w2c_lo = sbt(es_c, "w2c_lo", [128, FCQ * D], BF16)
    wstg = sbt(es_c, "wstg", [128, D], F32)
    xg = [sbt(es_c, f"xg{j}", [128, 2 * DC, 512], BF16) for j in range(5)]

    hf_sb = sbt(es_c, "hf_sb", [128, 256], F32)
    hh_sb = sbt(es_c, "hh_sb", [128, 256], BF16)
    hl_sb = sbt(es_c, "hl_sb", [128, 256], BF16)
    g_sb = sbt(es_c, "g_sb", [128, NTK * 2], F32)
    tk_i = sbt(es_c, "tk_i", [128, NTK * 2], I32)
    y_sb = sbt(es_c, "y_sb", [128, D], F32)
    idxw = sbt(es_c, "idxw", [128, C_LOC // 16], I16)
    idxg = sbt(es_c, "idxg", [128, 32], I16)
    pl_sb = sbt(es_c, "pl_sb", [128, 128], F32)

    def p2_expert(e_):
        c0 = (e_ * C_LOC) // 16
        PP = P2PARTS

        def idxcp(eng):
            eng.tensor_copy(out=idxw[:, :], in_=idx16[:, c0:c0 + C_LOC // 16])
            eng.drain()
            return eng.engine_nop()
        P.add("vector", idxcp)
        if "g" not in PP:
            return
        for j in range(5):
            def idxcp2(eng, j=j):
                eng.tensor_copy(out=idxg[:, :], in_=idxw[:, j * 32:(j + 1) * 32])
                eng.drain()
                return eng.engine_nop()
            P.add("vector", idxcp2)
            P.dma("gpsimd", lambda e, j=j: e.dma_gather(
                xg[j][:, :, :], xln2_cat_d[:, :], idxg[:, :],
                num_idxs=512, num_idxs_reg=512, elem_size=2 * D, transpose=True,
                single_packet=False))

        if "c" not in PP:
            return

        P.dma("sync", lambda e, e_=e_: e.dma_start(
            out=pl_sb[0:20, 0:128],
            in_=tokplane_d[e_ * C_LOC:(e_ + 1) * C_LOC].rearrange("(a b) -> a b", b=128)))
        P.dma("sync", lambda e, e_=e_: e.dma_start(
            out=pl_sb[20:40, 0:128],
            in_=gateplane_d[e_ * C_LOC:(e_ + 1) * C_LOC].rearrange("(a b) -> a b", b=128)))
        P.add("tensor", lambda e: e.transpose(psum[3][:, 0:40], pl_sb[0:40, 0:128],
                                              ident_f[0:40, 0:40]))

        def gates(eng, e_=e_):
            eng.tensor_copy(out=wrk[:, 0:20], in_=psum[3][:, 0:20])
            eng.drain()
            eng.tensor_copy(out=tk_i[:, :], in_=wrk[:, 0:20])
            eng.tensor_copy(out=g_sb[:, :], in_=psum[3][:, 20:40])
            eng.drain()
            for sub in range(NTK * 2):
                eng.tensor_scalar_add(wrk[:, 20:21], iotap[:, :], float(sub * 128))
                eng.drain()
                eng.tensor_tensor(wrk[:, 20:21], wrk[:, 20:21], tau_bc[:, e_:e_ + 1],
                                  op=AL.is_lt)
                eng.drain()
                eng.tensor_tensor(g_sb[:, sub:sub + 1], g_sb[:, sub:sub + 1],
                                  wrk[:, 20:21], op=AL.mult)
                eng.drain()
            return eng.engine_nop()
        P.add("vector", gates)

        if "w" not in PP:
            return
        for q in range(4):
            for fc in range(FCQ):
                fa = q * (FFN // 4) + fc * 128
                P.dma("sync", lambda e, e_=e_, fa=fa: e.dma_start(
                    out=wstg[:, :].rearrange("p (c f) -> p c f", c=DC),
                    in_=w1_in[e_].rearrange("(c p) f -> p c f", p=128)[:, :, fa:fa + 128]))
                P.add("scalar", lambda e, fc=fc: e.copy(out=w1c_hi[:, fc * D:(fc + 1) * D],
                                                        in_=wstg[:, :]))

                def wlo1(eng, fc=fc):
                    eng.tensor_tensor(out=w1c_lo[:, fc * D:(fc + 1) * D], in0=wstg[:, :],
                                      in1=w1c_hi[:, fc * D:(fc + 1) * D], op=AL.subtract)
                    eng.drain()
                    return eng.engine_nop()
                P.add("vector", wlo1)
                P.dma("sync", lambda e, e_=e_, fa=fa: e.dma_start(out=wstg[:, :],
                                                                  in_=w2_in[e_, fa:fa + 128, :]))
                P.add("scalar", lambda e, fc=fc: e.copy(out=w2c_hi[:, fc * D:(fc + 1) * D],
                                                        in_=wstg[:, :]))

                def wlo2(eng, fc=fc):
                    eng.tensor_tensor(out=w2c_lo[:, fc * D:(fc + 1) * D], in0=wstg[:, :],
                                      in1=w2c_hi[:, fc * D:(fc + 1) * D], op=AL.subtract)
                    eng.drain()
                    return eng.engine_nop()
                P.add("vector", wlo2)

            if "h" not in PP:
                continue
            for k in range(NTK):
                for fc in range(FCQ):
                    def hmm(eng, fc=fc, k=k):
                        bank = psum[fc % 2]
                        ins = None
                        for dc in range(DC):
                            lh = w1c_hi[:, fc * D + dc * 128:fc * D + (dc + 1) * 128]
                            ll = w1c_lo[:, fc * D + dc * 128:fc * D + (dc + 1) * 128]
                            rh = xg[k // 2][:, dc, (k % 2) * 256:(k % 2) * 256 + 256]
                            rl = xg[k // 2][:, DC + dc, (k % 2) * 256:(k % 2) * 256 + 256]
                            eng.matmul(bank[:, 0:256], lh, rh, start=(dc == 0), stop=False)
                            eng.matmul(bank[:, 0:256], lh, rl, start=False, stop=False)
                            ins = eng.matmul(bank[:, 0:256], ll, rh, start=False,
                                             stop=(dc == DC - 1))
                        return ins
                    P.add("tensor", hmm)
                    P.add("scalar", lambda e, fc=fc: e.activation(out=hf_sb[:, :],
                                                                  in_=psum[fc % 2][:, 0:256],
                                                                  func=AF.Gelu_apprx_tanh))
                    P.add("scalar", lambda e: e.copy(out=hh_sb[:, :], in_=hf_sb[:, :]))

                    def hlo(eng):
                        eng.tensor_tensor(out=hl_sb[:, :], in0=hf_sb[:, :], in1=hh_sb[:, :],
                                          op=AL.subtract)
                        eng.drain()
                        return eng.engine_nop()
                    P.add("vector", hlo)

                    def ymm(eng, fc=fc):
                        ins = None
                        for sub in range(2):
                            for dh in range(2):
                                bank = psum[4 + sub * 2 + dh]
                                lh = hh_sb[:, sub * 128:(sub + 1) * 128]
                                ll = hl_sb[:, sub * 128:(sub + 1) * 128]
                                rh = w2c_hi[:, fc * D + dh * 512:fc * D + (dh + 1) * 512]
                                rl = w2c_lo[:, fc * D + dh * 512:fc * D + (dh + 1) * 512]
                                eng.matmul(bank[:, :], lh, rh, start=(fc == 0), stop=False)
                                eng.matmul(bank[:, :], lh, rl, start=False, stop=False)
                                ins = eng.matmul(bank[:, :], ll, rh, start=False,
                                                 stop=(fc == FCQ - 1))
                        return ins
                    P.add("tensor", ymm)

                if "y" not in PP:
                    continue
                for sub in range(2):
                    def yev(eng, sub=sub, k=k):
                        for dh in range(2):
                            eng.tensor_scalar_mul(y_sb[:, dh * 512:(dh + 1) * 512],
                                                  psum[4 + sub * 2 + dh][:, :],
                                                  g_sb[:, 2 * k + sub:2 * k + sub + 1])
                        eng.drain()
                        return eng.engine_nop()
                    P.add("vector", yev)

                    if "s" not in PP:
                        continue

                    def yscat(eng, sub=sub, k=k):
                        return eng.indirect_dma_start(
                            out=out[:, :],
                            out_offset=IndirectOffsetOnAxis(tk_i[:, 2 * k + sub:2 * k + sub + 1], 0),
                            in_=y_sb[:, :], in_offset=None, compute_op=AL.add)
                    P.dma("gpsimd", yscat)

    if "p2" in phases:
        P.add("gpsimd", lambda e: (e.load_library(LIB_MLP), e.engine_nop())[-1])
        for e_ in range(min(NE, P2N)):
            p2_expert(e_)

    es_c.close()

    P.add("gpsimd", lambda e: e.engine_nop())

    block = es.enter_context(nc.Block())
    P.emit(block)
    es.close()
    mybir.codegen_inst_isa_subclasses(nc)
    return nc


def make_inputs(inputs, core):
    vv = np.stack([np.asarray(inputs[k], np.float32) for k in
                   ("L0_attn_u", "L0_attn_w", "L0_attn_tmk", "L0_attn_tmv", "L0_attn_tmr",
                    "L0_attn_bo")])
    mask = np.zeros((32, 5), np.float32)
    for bl in range(4):
        mask[: 4 * core + bl, bl] = 1.0
    mask[: 4 * core, 4] = 1.0
    return {
        "x": np.ascontiguousarray(np.asarray(inputs["L0_attn_x"], np.float32)[4 * core:4 * core + 4]),
        "Wk": np.asarray(inputs["L0_attn_Wk"], np.float32),
        "Wv": np.asarray(inputs["L0_attn_Wv"], np.float32),
        "Wr": np.asarray(inputs["L0_attn_Wr"], np.float32),
        "Wo": np.asarray(inputs["L0_attn_Wo"], np.float32),
        "vecs": vv,
        "w_router": np.asarray(inputs["L0_ffn_w_router"], np.float32),
        "w1": np.asarray(inputs["L0_ffn_w1"], np.float32),
        "w2": np.asarray(inputs["L0_ffn_w2"], np.float32),
        "bias": np.asarray(inputs["L0_ffn_bias"], np.float32).reshape(1, D),
        "maskT": mask,
    }


def kernel(**inputs):
    nc = build_program()
    in_maps = [make_inputs(inputs, core) for core in range(8)]
    res = run_bass_kernel_spmd(nc, in_maps, core_ids=list(range(8)))
    shards = [res.results[cu]["out"].reshape(B, T, D) for cu in range(8)]
    return np.concatenate(shards, axis=0)


# version 02
# speedup vs baseline: 1.4833x; 1.4833x over previous
import numpy as np
from contextlib import ExitStack

import concourse.bass as bass
import concourse.mybir as mybir
from concourse.bass_utils import run_bass_kernel_spmd
from concourse.bass_core_rust import IndirectOffsetOnAxis
from concourse.library_config import mlp as LIB_MLP

F32 = mybir.dt.float32
BF16 = mybir.dt.bfloat16
I16 = mybir.dt.int16
I32 = mybir.dt.int32
AF = mybir.ActivationFunctionType
AL = mybir.AluOpType
AX = mybir.AxisListType.X

D = 1024
T = 2048
B = 4
DC = 8
FFN = 4096
NE = 8
C_LOC = 2560
NSLOT = NE * C_LOC
NSLOT_PAD = 20608
CAP = 16384
CH1 = 256
CH2 = 512
DUMMY = B * T
P2N = 8
P2RELAX = "chgy"
P2PARTS = "gwchys"
ENGS = ("sync", "scalar", "vector", "tensor", "gpsimd")


class Ctr:
    def __init__(self, nc, es, name):
        self.sem = es.enter_context(nc.semaphore(name))
        self.name = name
        self.n = 0


class Prog:

    def __init__(self, nc, es):
        self.nc = nc
        self.es = es
        self.ctr = {e: Ctr(nc, es, f"c_{e}") for e in ENGS}
        self.steps = []
        self.seen = {e: {e2: 0 for e2 in ENGS} for e in ENGS}

    def new_ctr(self, name):
        self.ctr[name] = Ctr(self.nc, self.es, name)
        for e in ENGS:
            self.seen[e][name] = 0
        return name

    def add(self, eng, fn, inck=1):
        waits = []
        for e2 in self.ctr:
            v = self.ctr[e2].n
            if v > self.seen[eng].get(e2, 0):
                waits.append((self.ctr[e2].sem, v))
                self.seen[eng][e2] = v
        self.steps.append((eng, waits, fn, (self.ctr[eng].sem, inck)))
        self.ctr[eng].n += inck
        return self.ctr[eng].n

    def addm(self, eng, fn, mwaits, inck=1, ctr=None):
        waits = []
        for (en, v) in mwaits:
            if v is None:
                continue
            if v > self.seen[eng].get(en, 0):
                waits.append((self.ctr[en].sem, v))
                self.seen[eng][en] = v
        tgt = ctr if ctr is not None else eng
        self.steps.append((eng, waits, fn, (self.ctr[tgt].sem, inck)))
        self.ctr[tgt].n += inck
        return self.ctr[tgt].n

    def dma(self, eng, fn):
        return self.add(eng, fn, inck=16)

    def dmam(self, eng, fn, mwaits, ctr=None):
        return self.addm(eng, fn, mwaits, inck=16, ctr=ctr)

    def emit(self, block):
        for eng_name in ENGS:
            steps = [s for s in self.steps if s[0] == eng_name]

            def body(eng, steps=steps):
                for (_e, waits, fn, inc) in steps:
                    for (sem, v) in waits:
                        eng.wait_ge(sem, v)
                    ins = fn(eng)
                    if isinstance(ins, (list, tuple)):
                        ins = ins[-1]
                    assert ins is not None
                    ins.then_inc(inc[0], inc[1])
            getattr(block, eng_name)(body)


def build_program(phases=("p0", "p1a", "p1b", "p15", "p2"), debug=()):
    nc = bass.Bass("TRN2", target_bir_lowering=False, debug=False)
    es = ExitStack()

    def din(name, shape, dt=F32):
        return nc.dram_tensor(name, shape, dt, kind="ExternalInput").ap()

    x_in = din("x", [B, T, D])
    Wk_in = din("Wk", [D, D]); Wv_in = din("Wv", [D, D])
    Wr_in = din("Wr", [D, D]); Wo_in = din("Wo", [D, D])
    vecs_in = din("vecs", [6, D])
    wr_in = din("w_router", [D, NE])
    w1_in = din("w1", [NE, D, FFN])
    w2_in = din("w2", [NE, FFN, D])
    bias_in = din("bias", [1, D])
    maskT_in = din("maskT", [32, 5])

    out = nc.dram_tensor("out", [B * T, D], F32, kind="ExternalOutput").ap()

    def dscr(name, shape, dt):
        if name in debug:
            return nc.dram_tensor(name, shape, dt, kind="ExternalOutput").ap()
        return nc.dram_tensor(name, shape, dt).ap()

    rwkv_hi_d = dscr("rwkv_hi", [D, B * T], BF16)
    rwkv_lo_d = dscr("rwkv_lo", [D, B * T], BF16)
    xln2_cat_d = dscr("xln2_cat", [B * T + 1, 2 * D], BF16)
    tokplane_d = dscr("tokplane", [NSLOT_PAD + 1], F32)
    gateplane_d = dscr("gateplane", [NSLOT_PAD + 1], F32)
    cc_in_d = nc.dram_tensor("cc_in", [B, NE], F32).ap()
    cc_out_d = nc.dram_tensor("cc_out", [32, NE], F32).ap()
    cc_dbg_d = (nc.dram_tensor("cc_dbg", [32, NE], F32, kind="ExternalOutput").ap()
                if "cc_out" in debug else None)

    def sbt(stack, name, shape, dt):
        return stack.enter_context(nc.sbuf_tensor(name, shape, dt))

    ident_f = sbt(es, "ident_f", [128, 128], F32)
    ones128 = sbt(es, "ones128", [128, 128], F32)
    Lstr_f = sbt(es, "Lstr_f", [128, 128], F32)
    iota8 = sbt(es, "iota8", [128, NE], F32)
    iotap = sbt(es, "iotap", [128, 1], F32)
    vecs = sbt(es, "vecs_sb", [128, 6, DC], F32)
    omt = sbt(es, "omt", [128, 3, DC], F32)
    negw = sbt(es, "negw", [128, DC], F32)
    wrh = sbt(es, "wrh", [128, DC, NE], BF16)
    wrl = sbt(es, "wrl", [128, DC, NE], BF16)
    cs_bc = sbt(es, "cs_bc", [128, NE], F32)
    bias_bc = sbt(es, "bias_bc", [128, D], F32)
    stat = sbt(es, "stat", [128, 16], F32)
    wrk = sbt(es, "wrk", [128, 32], F32)
    wrk_i = sbt(es, "wrk_i", [128, 4], I32)
    meta_sb = sbt(es, "meta_sb", [128, 64, 6], F32)
    carry = sbt(es, "carry", [128, NE], F32)
    idx16 = sbt(es, "idx16", [128, 1280], I16)
    tau_bc = sbt(es, "tau_bc", [128, NE], F32)
    lcb_bc = sbt(es, "lcb_bc", [128, B, NE], F32)

    psum = [es.enter_context(nc.psum_tensor(f"ps{i}", [128, 512], F32)) for i in range(8)]

    P = Prog(nc, es)

    es_a = ExitStack()
    wk_hi = sbt(es_a, "wk_hi", [128, DC * D], BF16); wk_lo = sbt(es_a, "wk_lo", [128, DC * D], BF16)
    wv_hi = sbt(es_a, "wv_hi", [128, DC * D], BF16); wv_lo = sbt(es_a, "wv_lo", [128, DC * D], BF16)
    wq_hi = sbt(es_a, "wq_hi", [128, DC * D], BF16); wq_lo = sbt(es_a, "wq_lo", [128, DC * D], BF16)
    x_tok = sbt(es_a, "x_tok", [128, 2 * D], F32)
    sq_scr = sbt(es_a, "sq_scr", [128, D], BF16)
    xln_fm = sbt(es_a, "xln_fm", [128, DC * (CH1 + 1)], F32)
    mix_f = sbt(es_a, "mix_f", [128, DC * CH1], F32)
    mixh = sbt(es_a, "mixh", [128, DC * CH1], BF16)
    mixl = sbt(es_a, "mixl", [128, DC * CH1], BF16)
    k_sb = sbt(es_a, "k_sb", [128, DC * CH1], F32)
    v_sb = sbt(es_a, "v_sb", [128, DC * CH1], F32)
    r_sb = sbt(es_a, "r_sb", [128, DC * CH1], F32)
    e1_sb = sbt(es_a, "e1_sb", [128, DC * CH1], F32)
    e2_sb = sbt(es_a, "e2_sb", [128, DC * CH1], F32)
    a_sb = sbt(es_a, "a_sb", [128, DC * CH1], F32)
    b_sb = sbt(es_a, "b_sb", [128, DC * CH1], F32)
    st_a = sbt(es_a, "st_a", [128, DC], F32)
    st_b = sbt(es_a, "st_b", [128, DC], F32)
    mixf_b = mix_f[:, :].bitcast(BF16)
    rwh_sb = mixf_b[:, 0:DC * CH1]
    rwl_sb = mixf_b[:, DC * CH1:2 * DC * CH1]

    def p0_consts(eng):
        eng.memset(ident_f[:, :], 1.0)
        eng.memset(ones128[:, :], 1.0)
        eng.memset(Lstr_f[:, :], 1.0)
        eng.drain()
        eng.affine_select(ident_f[:, :], ident_f[:, :], pattern=[[-1, 128]],
                          compare_op=AL.is_equal, fill=0.0, base=0, channel_multiplier=1)
        eng.affine_select(Lstr_f[:, :], Lstr_f[:, :], pattern=[[1, 128]],
                          compare_op=AL.is_gt, fill=0.0, base=0, channel_multiplier=-1)
        eng.iota(iota8[:, :], pattern=[[1, NE]], base=0, channel_multiplier=0,
                 allow_small_or_imprecise_dtypes=True)
        return eng.iota(iotap[:, :], pattern=[[1, 1]], base=0, channel_multiplier=1,
                        allow_small_or_imprecise_dtypes=True)
    P.add("gpsimd", p0_consts)

    P.dma("sync", lambda e: e.dma_start(
        out=x_tok[0:48, 0:128],
        in_=vecs_in.rearrange("a (c p) -> (a c) p", p=128)))
    P.add("tensor", lambda e: e.transpose(psum[0][:, 0:48], x_tok[0:48, 0:128], ident_f[0:48, 0:48]))

    def p0_vecs(eng):
        eng.tensor_copy(out=vecs[:, :, :].rearrange("p j c -> p (j c)"), in_=psum[0][:, 0:48])
        eng.drain()
        for j in range(3):
            eng.tensor_scalar(omt[:, j, :], vecs[:, 2 + j, :], -1.0, 1.0,
                              op0=AL.mult, op1=AL.add)
        eng.tensor_scalar_mul(negw[:, :], vecs[:, 1, :], -1.0)
        eng.drain()
        return eng.engine_nop()
    P.add("vector", p0_vecs)

    P.dma("sync", lambda e: e.dma_start(
        out=mix_f[:, 0:DC * NE].rearrange("p (c e) -> p c e", c=DC),
        in_=wr_in.rearrange("(c p) e -> p c e", p=128)))

    def p0_wr(eng):
        eng.tensor_copy(out=wrh[:, :, :].rearrange("p c e -> p (c e)"), in_=mix_f[:, 0:DC * NE])
        eng.drain()
        eng.tensor_tensor(out=wrl[:, :, :].rearrange("p c e -> p (c e)"),
                          in0=mix_f[:, 0:DC * NE],
                          in1=wrh[:, :, :].rearrange("p c e -> p (c e)"), op=AL.subtract)
        eng.drain()
        return eng.engine_nop()
    P.add("vector", p0_wr)

    def p0_cs(eng):
        ins = None
        for dcq in range(DC):
            ins = eng.matmul(psum[1][0:1, 0:NE], ones128[:, 0:1],
                             mix_f[:, dcq * NE:(dcq + 1) * NE],
                             start=(dcq == 0), stop=(dcq == DC - 1))
        return ins
    P.add("tensor", p0_cs)
    P.add("vector", lambda e: e.tensor_copy(out=stat[0:1, 0:NE], in_=psum[1][0:1, 0:NE]))
    P.add("tensor", lambda e: e.matmul(psum[2][:, 0:NE], ones128[0:1, :], stat[0:1, 0:NE],
                                       start=True, stop=True))
    P.add("vector", lambda e: e.tensor_copy(out=cs_bc[:, :], in_=psum[2][:, 0:NE]))

    P.dma("sync", lambda e: e.dma_start(out=x_tok[0:1, 0:D], in_=bias_in[:, :]))

    def p0_bias(eng):
        eng.matmul(psum[1][:, :], ones128[0:1, :], x_tok[0:1, 0:512], start=True, stop=True)
        return eng.matmul(psum[2][:, :], ones128[0:1, :], x_tok[0:1, 512:1024],
                          start=True, stop=True)
    P.add("tensor", p0_bias)

    def p0_bias_ev(eng):
        eng.tensor_copy(out=bias_bc[:, 0:512], in_=psum[1][:, :])
        eng.tensor_copy(out=bias_bc[:, 512:1024], in_=psum[2][:, :])
        eng.drain()
        return eng.engine_nop()
    P.add("vector", p0_bias_ev)

    def prep_attnw(W_in, w_hi, w_lo, stg, scr):
        for rc in range(DC):
            P.dma("sync", lambda e, rc=rc: e.dma_start(out=stg[:, 0:D],
                                                       in_=W_in[rc * 128:(rc + 1) * 128, :]))
            for half in range(2):
                def tr(eng, half=half):
                    ins = None
                    for j in range(4):
                        ccd = half * 4 + j
                        ins = eng.transpose(psum[4 + j][:, 0:128],
                                            stg[:, ccd * 128:(ccd + 1) * 128], ident_f[:, :])
                    return ins
                P.add("tensor", tr)

                def ev(eng, rc=rc, half=half, w_hi=w_hi, w_lo=w_lo):
                    for j in range(4):
                        eng.tensor_copy(out=scr[:, j * 128:(j + 1) * 128], in_=psum[4 + j][:, 0:128])
                    eng.drain()
                    for j in range(4):
                        o = (half * 4 + j) * D + rc * 128
                        eng.tensor_copy(out=w_hi[:, o:o + 128], in_=scr[:, j * 128:(j + 1) * 128])
                    eng.drain()
                    for j in range(4):
                        o = (half * 4 + j) * D + rc * 128
                        eng.tensor_tensor(out=w_lo[:, o:o + 128],
                                          in0=scr[:, j * 128:(j + 1) * 128],
                                          in1=w_hi[:, o:o + 128], op=AL.subtract)
                    eng.drain()
                    return eng.engine_nop()
                P.add("vector", ev)

    if "p1a" in phases:
        prep_attnw(Wk_in, wk_hi, wk_lo, k_sb, v_sb)
        prep_attnw(Wv_in, wv_hi, wv_lo, k_sb, v_sb)
        prep_attnw(Wr_in, wq_hi, wq_lo, k_sb, v_sb)

    def p1a_chunk(b, ci):
        t0 = ci * CH1
        first = (ci == 0)
        P.dma("sync", lambda e: e.dma_start(out=x_tok[:, 0:D], in_=x_in[b, t0:t0 + 128, :]))
        P.dma("sync", lambda e: e.dma_start(out=x_tok[:, D:2 * D],
                                            in_=x_in[b, t0 + 128:t0 + 256, :]))

        def ln_sums(eng):
            for ti in range(2):
                eng.reduce_sum(stat[:, ti * 8:ti * 8 + 1], x_tok[:, ti * D:(ti + 1) * D], axis=AX)
            eng.drain()
            return eng.engine_nop()
        P.add("vector", ln_sums)

        def ln_sq(eng):
            ins = None
            for ti in range(2):
                if ti:
                    eng.drain()
                ins = eng.activation(out=sq_scr[:, :], in_=x_tok[:, ti * D:(ti + 1) * D],
                                     func=AF.Square, accum_out=stat[:, ti * 8 + 2:ti * 8 + 3])
            return ins
        P.add("scalar", ln_sq)

        def ln_var(eng):
            for ti in range(2):
                s = stat[:, ti * 8:(ti + 1) * 8]
                eng.tensor_scalar_mul(s[:, 1:2], s[:, 0:1], 1.0 / D)
                eng.tensor_scalar_mul(s[:, 3:4], s[:, 2:3], 1.0 / D)
            eng.drain()
            for ti in range(2):
                s = stat[:, ti * 8:(ti + 1) * 8]
                eng.tensor_tensor(s[:, 4:5], s[:, 1:2], s[:, 1:2], op=AL.mult)
            eng.drain()
            for ti in range(2):
                s = stat[:, ti * 8:(ti + 1) * 8]
                eng.tensor_tensor(s[:, 5:6], s[:, 3:4], s[:, 4:5], op=AL.subtract)
            eng.drain()
            for ti in range(2):
                s = stat[:, ti * 8:(ti + 1) * 8]
                eng.tensor_scalar_add(s[:, 5:6], s[:, 5:6], 1e-5)
            eng.drain()
            return eng.engine_nop()
        P.add("vector", ln_var)

        def ln_sqrt(eng):
            ins = None
            for ti in range(2):
                s = stat[:, ti * 8:(ti + 1) * 8]
                ins = eng.activation(out=s[:, 6:7], in_=s[:, 5:6], func=AF.Sqrt)
            return ins
        P.add("scalar", ln_sqrt)

        def ln_norm(eng):
            for ti in range(2):
                s = stat[:, ti * 8:(ti + 1) * 8]
                eng.reciprocal(s[:, 7:8], s[:, 6:7])
            eng.drain()
            for ti in range(2):
                s = stat[:, ti * 8:(ti + 1) * 8]
                eng.tensor_scalar(x_tok[:, ti * D:(ti + 1) * D], x_tok[:, ti * D:(ti + 1) * D],
                                  s[:, 1:2], s[:, 7:8], op0=AL.subtract, op1=AL.mult)
            eng.drain()
            return eng.engine_nop()
        P.add("vector", ln_norm)

        def shift_prep(eng):
            for dc in range(DC):
                o = dc * (CH1 + 1)
                if first:
                    eng.memset(xln_fm[:, o:o + 1], 0.0)
                else:
                    eng.tensor_copy(out=xln_fm[:, o:o + 1], in_=xln_fm[:, o + CH1:o + CH1 + 1])
            eng.drain()
            return eng.engine_nop()
        P.add("vector", shift_prep)

        for ti in range(2):
            for g in range(2):
                def tr(eng, ti=ti, g=g):
                    ins = None
                    for j in range(4):
                        dc = g * 4 + j
                        ins = eng.transpose(psum[4 + j][:, 0:128],
                                            x_tok[:, ti * D + dc * 128:ti * D + (dc + 1) * 128],
                                            ident_f[:, :])
                    return ins
                P.add("tensor", tr)

                def ev(eng, ti=ti, g=g):
                    for j in range(4):
                        dc = g * 4 + j
                        o = dc * (CH1 + 1) + 1 + ti * 128
                        eng.tensor_copy(out=xln_fm[:, o:o + 128], in_=psum[4 + j][:, 0:128])
                    eng.drain()
                    return eng.engine_nop()
                P.add("vector", ev)

        projs = ((k_sb, wk_hi, wk_lo, 2), (v_sb, wv_hi, wv_lo, 3), (r_sb, wq_hi, wq_lo, 4))
        for dst, w_hi, w_lo, vj in projs:
            def mix(eng, vj=vj):
                for dc in range(DC):
                    o = dc * (CH1 + 1)
                    eng.tensor_scalar_mul(mix_f[:, dc * CH1:(dc + 1) * CH1],
                                          xln_fm[:, o:o + CH1], omt[:, vj - 2, dc:dc + 1])
                eng.drain()
                for dc in range(DC):
                    o = dc * (CH1 + 1)
                    eng.scalar_tensor_tensor(mix_f[:, dc * CH1:(dc + 1) * CH1],
                                             xln_fm[:, o + 1:o + 1 + CH1],
                                             vecs[:, vj, dc:dc + 1],
                                             mix_f[:, dc * CH1:(dc + 1) * CH1],
                                             op0=AL.mult, op1=AL.add)
                eng.drain()
                return eng.engine_nop()
            P.add("vector", mix)
            if vj == 4:
                P.add("scalar", lambda e: e.activation(out=mix_f[:, :], in_=mix_f[:, :],
                                                       func=AF.Sigmoid))
            P.add("scalar", lambda e: e.copy(out=mixh[:, :], in_=mix_f[:, :]))

            def lo(eng):
                eng.tensor_tensor(out=mixl[:, :], in0=mix_f[:, :], in1=mixh[:, :], op=AL.subtract)
                eng.drain()
                return eng.engine_nop()
            P.add("vector", lo)

            for ec in range(DC):
                def pj(eng, ec=ec, w_hi=w_hi, w_lo=w_lo):
                    bank = psum[ec % 2]
                    ins = None
                    for dc in range(DC):
                        lh = w_hi[:, dc * D + ec * 128:dc * D + ec * 128 + 128]
                        ll = w_lo[:, dc * D + ec * 128:dc * D + ec * 128 + 128]
                        rh = mixh[:, dc * CH1:(dc + 1) * CH1]
                        rl = mixl[:, dc * CH1:(dc + 1) * CH1]
                        eng.matmul(bank[:, 0:CH1], lh, rh, start=(dc == 0), stop=False)
                        eng.matmul(bank[:, 0:CH1], lh, rl, start=False, stop=False)
                        ins = eng.matmul(bank[:, 0:CH1], ll, rh, start=False, stop=(dc == DC - 1))
                    return ins
                P.add("tensor", pj)
                P.add("vector", lambda e, ec=ec, dst=dst: e.tensor_copy(
                    out=dst[:, ec * CH1:(ec + 1) * CH1], in_=psum[ec % 2][:, 0:CH1]))

        def sprep(eng):
            for dc in range(DC):
                sl = k_sb[:, dc * CH1:(dc + 1) * CH1]
                eng.tensor_scalar_add(sl, sl, vecs[:, 0, dc:dc + 1])
            eng.drain()
            for dc in range(DC):
                sl = k_sb[:, dc * CH1:(dc + 1) * CH1]
                eng.tensor_scalar(e2_sb[:, dc * CH1:(dc + 1) * CH1], sl,
                                  vecs[:, 1, dc:dc + 1], 0.0, op0=AL.subtract, op1=AL.min)
                eng.tensor_single_scalar(e1_sb[:, dc * CH1:(dc + 1) * CH1], sl,
                                         vecs[:, 1, dc:dc + 1], op=AL.max)
            eng.drain()
            return eng.engine_nop()
        P.add("vector", sprep)

        def sexp(eng):
            for dc in range(DC):
                eng.activation(out=e1_sb[:, dc * CH1:(dc + 1) * CH1],
                               in_=e1_sb[:, dc * CH1:(dc + 1) * CH1], func=AF.Exp,
                               bias=negw[:, dc:dc + 1], scale=-1.0)
            ins = None
            for dc in range(DC):
                ins = eng.activation(out=e2_sb[:, dc * CH1:(dc + 1) * CH1],
                                     in_=e2_sb[:, dc * CH1:(dc + 1) * CH1], func=AF.Exp)
            return ins
        P.add("scalar", sexp)

        def scans(eng):
            eng.tensor_tensor(out=v_sb[:, :], in0=e2_sb[:, :], in1=v_sb[:, :], op=AL.mult)
            if first:
                eng.memset(st_a[:, :], 0.0)
                eng.memset(st_b[:, :], 0.0)
            eng.drain()
            for dc in range(DC):
                eng.tensor_tensor_scan(a_sb[:, dc * CH1:(dc + 1) * CH1],
                                       e1_sb[:, dc * CH1:(dc + 1) * CH1],
                                       v_sb[:, dc * CH1:(dc + 1) * CH1],
                                       st_a[:, dc:dc + 1], AL.mult, AL.add)
                eng.tensor_tensor_scan(b_sb[:, dc * CH1:(dc + 1) * CH1],
                                       e1_sb[:, dc * CH1:(dc + 1) * CH1],
                                       e2_sb[:, dc * CH1:(dc + 1) * CH1],
                                       st_b[:, dc:dc + 1], AL.mult, AL.add)
            eng.drain()
            eng.tensor_copy(out=st_a[:, :],
                            in_=a_sb[:, :].rearrange("p (c t) -> p c t", c=DC)[:, :, CH1 - 1])
            eng.tensor_copy(out=st_b[:, :],
                            in_=b_sb[:, :].rearrange("p (c t) -> p c t", c=DC)[:, :, CH1 - 1])
            eng.drain()
            eng.reciprocal(b_sb[:, :], b_sb[:, :])
            eng.drain()
            eng.tensor_tensor(out=a_sb[:, :], in0=a_sb[:, :], in1=b_sb[:, :], op=AL.mult)
            eng.drain()
            eng.tensor_tensor(out=a_sb[:, :], in0=a_sb[:, :], in1=r_sb[:, :], op=AL.mult)
            eng.drain()
            return eng.engine_nop()
        P.add("vector", scans)

        P.add("scalar", lambda e: e.copy(out=rwh_sb, in_=a_sb[:, :]))

        def rlo(eng):
            eng.tensor_tensor(out=rwl_sb, in0=a_sb[:, :], in1=rwh_sb, op=AL.subtract)
            eng.drain()
            return eng.engine_nop()
        P.add("vector", rlo)

        if b == 0 and ci == 0 and "dbg1" in debug:
            for nm, buf in (("dbg_xlnfm", xln_fm), ("dbg_k", k_sb), ("dbg_v", v_sb),
                            ("dbg_r", r_sb), ("dbg_e1", e1_sb), ("dbg_e2", e2_sb),
                            ("dbg_a", a_sb)):
                dd = nc.dram_tensor(nm, [128, buf.shape[1] if hasattr(buf, 'shape') else 0],
                                    F32, kind="ExternalOutput").ap()
                P.dma("scalar", lambda e, dd=dd, buf=buf: e.dma_start(out=dd[:, :], in_=buf[:, :]))
        col0 = b * T + t0
        P.dma("scalar", lambda e: e.dma_start(
            out=rwkv_hi_d.rearrange("(c p) t -> p c t", p=128)[:, :, col0:col0 + CH1],
            in_=rwh_sb.rearrange("p (c t) -> p c t", c=DC)))
        P.dma("scalar", lambda e: e.dma_start(
            out=rwkv_lo_d.rearrange("(c p) t -> p c t", p=128)[:, :, col0:col0 + CH1],
            in_=rwl_sb.rearrange("p (c t) -> p c t", c=DC)))

    if "p1a" in phases:
        for b in range(B):
            for ci in range(T // CH1):
                p1a_chunk(b, ci)

    es_a.close()

    es_b = ExitStack()
    wo_hi = sbt(es_b, "wo_hi", [128, DC * D], BF16)
    wo_lo = sbt(es_b, "wo_lo", [128, DC * D], BF16)
    stg_b = sbt(es_b, "stg_b", [128, D], F32)
    scr_b = sbt(es_b, "scr_b", [128, 512], F32)
    rwvh_sb = sbt(es_b, "rwvh_sb", [128, DC, CH2], BF16)
    rwvl_sb = sbt(es_b, "rwvl_sb", [128, DC, CH2], BF16)
    x4_sb = sbt(es_b, "x4_sb", [128, 4 * D], F32)
    ao_fm = sbt(es_b, "ao_fm", [128, DC * CH2], F32)
    xaf_hi = sbt(es_b, "xaf_hi", [128, DC * CH2], BF16)
    xaf_lo = sbt(es_b, "xaf_lo", [128, DC * CH2], BF16)
    xat_sb = sbt(es_b, "xat_sb", [128, 4 * D], F32)
    raw_sb = sbt(es_b, "raw_sb", [NE, CH2], F32)
    lgt_sb = sbt(es_b, "lgt_sb", [128, 4 * NE], F32)
    sc_sb = sbt(es_b, "sc_sb", [128, 4 * NE], F32)
    sq2_scr = sbt(es_b, "sq2_scr", [128, D], BF16)
    xl2 = sbt(es_b, "xl2", [128, D], F32)
    xl2h = sbt(es_b, "xl2h", [128, D], BF16)
    xl2l = sbt(es_b, "xl2l", [128, D], BF16)

    def p1b_chunk(b, ci):
        t0 = ci * CH2
        col0 = b * T + t0
        P.dma("sync", lambda e: e.dma_start(
            out=rwvh_sb[:, :, :],
            in_=rwkv_hi_d.rearrange("(c p) t -> p c t", p=128)[:, :, col0:col0 + CH2]))
        P.dma("sync", lambda e: e.dma_start(
            out=rwvl_sb[:, :, :],
            in_=rwkv_lo_d.rearrange("(c p) t -> p c t", p=128)[:, :, col0:col0 + CH2]))
        for ti in range(4):
            P.dma("sync", lambda e, ti=ti: e.dma_start(
                out=x4_sb[:, ti * D:(ti + 1) * D],
                in_=x_in[b, t0 + ti * 128:t0 + (ti + 1) * 128, :]))

        for oc in range(DC):
            def pj(eng, oc=oc):
                bank = psum[oc % 2]
                ins = None
                for dc in range(DC):
                    lh = wo_hi[:, dc * D + oc * 128:dc * D + oc * 128 + 128]
                    ll = wo_lo[:, dc * D + oc * 128:dc * D + oc * 128 + 128]
                    rh = rwvh_sb[:, dc, :]
                    rl = rwvl_sb[:, dc, :]
                    eng.matmul(bank[:, :], lh, rh, start=(dc == 0), stop=False)
                    eng.matmul(bank[:, :], lh, rl, start=False, stop=False)
                    ins = eng.matmul(bank[:, :], ll, rh, start=False, stop=(dc == DC - 1))
                return ins
            P.add("tensor", pj)
            P.add("scalar", lambda e, oc=oc: e.activation(
                out=ao_fm[:, oc * CH2:(oc + 1) * CH2], in_=psum[oc % 2][:, :],
                func=AF.Identity, bias=vecs[:, 5, oc:oc + 1]))

        for ti in range(4):
            for g in range(2):
                def tr(eng, ti=ti, g=g):
                    ins = None
                    for j in range(4):
                        dc = g * 4 + j
                        ins = eng.transpose(psum[4 + j][:, 0:128],
                                            x4_sb[:, ti * D + dc * 128:ti * D + (dc + 1) * 128],
                                            ident_f[:, :])
                    return ins
                P.add("tensor", tr)

                def ev(eng, ti=ti, g=g):
                    for j in range(4):
                        dc = g * 4 + j
                        o = dc * CH2 + ti * 128
                        eng.scalar_tensor_tensor(ao_fm[:, o:o + 128], psum[4 + j][:, 0:128],
                                                 1.0, ao_fm[:, o:o + 128],
                                                 op0=AL.mult, op1=AL.add)
                    eng.drain()
                    return eng.engine_nop()
                P.add("vector", ev)

        P.add("scalar", lambda e: e.copy(out=xaf_hi[:, :], in_=ao_fm[:, :]))

        def xlo(eng):
            eng.tensor_tensor(out=xaf_lo[:, :], in0=ao_fm[:, :], in1=xaf_hi[:, :], op=AL.subtract)
            eng.drain()
            return eng.engine_nop()
        P.add("vector", xlo)

        def rawmm(eng):
            bank = psum[2]
            ins = None
            for dc in range(DC):
                eng.matmul(bank[0:NE, 0:CH2], wrh[:, dc, :], xaf_hi[:, dc * CH2:(dc + 1) * CH2],
                           start=(dc == 0), stop=False)
                eng.matmul(bank[0:NE, 0:CH2], wrh[:, dc, :], xaf_lo[:, dc * CH2:(dc + 1) * CH2],
                           start=False, stop=False)
                ins = eng.matmul(bank[0:NE, 0:CH2], wrl[:, dc, :],
                                 xaf_hi[:, dc * CH2:(dc + 1) * CH2],
                                 start=False, stop=(dc == DC - 1))
            return ins
        P.add("tensor", rawmm)
        P.add("vector", lambda e: e.tensor_copy(out=raw_sb[:, :], in_=psum[2][0:NE, 0:CH2]))

        for ti in range(4):
            for g in range(2):
                def tr2(eng, ti=ti, g=g):
                    ins = None
                    for j in range(4):
                        dc = g * 4 + j
                        o = dc * CH2 + ti * 128
                        ins = eng.transpose(psum[4 + j][:, 0:128], ao_fm[:, o:o + 128],
                                            ident_f[:, :])
                    return ins
                P.add("tensor", tr2)

                def ev2(eng, ti=ti, g=g):
                    for j in range(4):
                        dc = g * 4 + j
                        eng.tensor_copy(out=xat_sb[:, ti * D + dc * 128:ti * D + (dc + 1) * 128],
                                        in_=psum[4 + j][:, 0:128])
                    eng.drain()
                    return eng.engine_nop()
                P.add("vector", ev2)

        def trraw(eng):
            ins = None
            for ti in range(4):
                ins = eng.transpose(psum[4 + ti][:, 0:NE], raw_sb[:, ti * 128:(ti + 1) * 128],
                                    ident_f[0:NE, 0:NE])
            return ins
        P.add("tensor", trraw)

        def evraw(eng):
            for ti in range(4):
                eng.tensor_copy(out=lgt_sb[:, ti * NE:(ti + 1) * NE], in_=psum[4 + ti][:, 0:NE])
            eng.drain()
            return eng.engine_nop()
        P.add("vector", evraw)

        for ti in range(4):
            trow = col0 + ti * 128
            tile_idx = b * (T // 128) + ci * 4 + ti
            xt = xat_sb[:, ti * D:(ti + 1) * D]

            def ln2a(eng, xt=xt):
                eng.reduce_sum(stat[:, 0:1], xt, axis=AX)
                eng.drain()
                return eng.engine_nop()
            P.add("vector", ln2a)
            P.add("scalar", lambda e, xt=xt: e.activation(out=sq2_scr[:, :], in_=xt,
                                                          func=AF.Square, accum_out=stat[:, 2:3]))

            def ln2b(eng):
                eng.tensor_scalar_mul(stat[:, 1:2], stat[:, 0:1], 1.0 / D)
                eng.tensor_scalar_mul(stat[:, 3:4], stat[:, 2:3], 1.0 / D)
                eng.drain()
                eng.tensor_tensor(stat[:, 4:5], stat[:, 1:2], stat[:, 1:2], op=AL.mult)
                eng.drain()
                eng.tensor_tensor(stat[:, 5:6], stat[:, 3:4], stat[:, 4:5], op=AL.subtract)
                eng.drain()
                eng.tensor_scalar_add(stat[:, 5:6], stat[:, 5:6], 1e-5)
                eng.drain()
                return eng.engine_nop()
            P.add("vector", ln2b)
            P.add("scalar", lambda e: e.activation(out=stat[:, 6:7], in_=stat[:, 5:6], func=AF.Sqrt))

            def ln2c(eng, xt=xt):
                eng.reciprocal(stat[:, 7:8], stat[:, 6:7])
                eng.drain()
                eng.tensor_scalar(xl2[:, :], xt, stat[:, 1:2], stat[:, 7:8],
                                  op0=AL.subtract, op1=AL.mult)
                eng.drain()
                eng.tensor_tensor(out=xt, in0=xt, in1=bias_bc[:, :], op=AL.add)
                eng.drain()
                return eng.engine_nop()
            P.add("vector", ln2c)
            P.add("scalar", lambda e: e.copy(out=xl2h[:, :], in_=xl2[:, :]))

            def xl2lo(eng):
                eng.tensor_tensor(out=xl2l[:, :], in0=xl2[:, :], in1=xl2h[:, :], op=AL.subtract)
                eng.drain()
                return eng.engine_nop()
            P.add("vector", xl2lo)
            P.dma("scalar", lambda e, trow=trow: e.dma_start(
                out=xln2_cat_d[trow:trow + 128, 0:D], in_=xl2h[:, :]))
            P.dma("scalar", lambda e, trow=trow: e.dma_start(
                out=xln2_cat_d[trow:trow + 128, D:2 * D], in_=xl2l[:, :]))
            P.dma("scalar", lambda e, trow=trow, xt=xt: e.dma_start(out=out[trow:trow + 128, :],
                                                                    in_=xt))

            def smax(eng, ti=ti):
                sc = sc_sb[:, ti * NE:(ti + 1) * NE]
                eng.scalar_tensor_tensor(sc, cs_bc[:, :], stat[:, 1:2],
                                         lgt_sb[:, ti * NE:(ti + 1) * NE],
                                         op0=AL.mult, op1=AL.subtract)
                eng.tensor_scalar_mul(stat[:, 8:9], stat[:, 7:8], -1.0)
                eng.drain()
                eng.tensor_scalar_mul(sc, sc, stat[:, 8:9])
                eng.drain()
                eng.reduce_max(wrk[:, 0:1], sc, axis=AX)
                eng.drain()
                eng.tensor_scalar_mul(wrk[:, 1:2], wrk[:, 0:1], -1.0)
                eng.drain()
                return eng.engine_nop()
            P.add("vector", smax)
            P.add("scalar", lambda e, ti=ti: e.activation(
                out=sc_sb[:, ti * NE:(ti + 1) * NE], in_=sc_sb[:, ti * NE:(ti + 1) * NE],
                func=AF.Exp, bias=wrk[:, 1:2]))

            def smax2(eng, ti=ti, tile_idx=tile_idx):
                sc = sc_sb[:, ti * NE:(ti + 1) * NE]
                m = meta_sb[:, tile_idx, :]
                eng.reduce_sum(wrk[:, 2:3], sc, axis=AX)
                eng.drain()
                eng.reciprocal(wrk[:, 3:4], wrk[:, 2:3])
                eng.drain()
                eng.tensor_scalar_mul(sc, sc, wrk[:, 3:4])
                eng.drain()
                eng.reduce_max(m[:, 0:1], sc, axis=AX)
                eng.drain()
                eng.tensor_scalar(wrk[:, 4:12], sc, m[:, 0:1], None, op0=AL.is_equal)
                eng.drain()
                eng.scalar_tensor_tensor(wrk[:, 12:20], wrk[:, 4:12], -1e30, sc,
                                         op0=AL.mult, op1=AL.add)
                eng.drain()
                eng.reduce_max(m[:, 1:2], wrk[:, 12:20], axis=AX)
                eng.drain()
                eng.tensor_scalar(wrk[:, 12:20], wrk[:, 12:20], m[:, 1:2], None, op0=AL.is_equal)
                eng.drain()
                eng.tensor_tensor(wrk[:, 20:28], wrk[:, 4:12], iota8[:, :], op=AL.mult)
                eng.drain()
                eng.reduce_sum(m[:, 2:3], wrk[:, 20:28], axis=AX)
                eng.drain()
                eng.tensor_tensor(wrk[:, 20:28], wrk[:, 12:20], iota8[:, :], op=AL.mult)
                eng.drain()
                eng.reduce_sum(m[:, 3:4], wrk[:, 20:28], axis=AX)
                eng.drain()
                eng.tensor_tensor(wrk[:, 4:12], wrk[:, 4:12], wrk[:, 12:20], op=AL.add)
                eng.drain()
                return eng.engine_nop()
            P.add("vector", smax2)

            if ci == 0 and ti == 0:
                def czero(eng):
                    eng.memset(carry[:, :], 0.0)
                    eng.drain()
                    return eng.engine_nop()
                P.add("vector", czero)

            def pref(eng):
                eng.matmul(psum[3][:, 0:NE], Lstr_f[:, :], wrk[:, 4:12], start=True, stop=False)
                eng.matmul(psum[3][:, 0:NE], ones128[0:1, :], carry[0:1, :],
                           start=False, stop=True)
                eng.matmul(psum[2][0:1, 0:NE], ones128[:, 0:1], wrk[:, 4:12],
                           start=True, stop=False)
                return eng.matmul(psum[2][0:1, 0:NE], ones128[0:1, 0:1], carry[0:1, :],
                                  start=False, stop=True)
            P.add("tensor", pref)

            def pref_ev(eng, tile_idx=tile_idx):
                m = meta_sb[:, tile_idx, :]
                eng.tensor_copy(out=wrk[:, 12:20], in_=psum[3][:, 0:NE])
                eng.tensor_copy(out=carry[0:1, :], in_=psum[2][0:1, 0:NE])
                eng.drain()
                eng.tensor_scalar(wrk[:, 20:28], iota8[:, :], m[:, 2:3], None, op0=AL.is_equal)
                eng.drain()
                eng.tensor_tensor(wrk[:, 20:28], wrk[:, 20:28], wrk[:, 12:20], op=AL.mult)
                eng.drain()
                eng.reduce_sum(m[:, 4:5], wrk[:, 20:28], axis=AX)
                eng.drain()
                eng.tensor_scalar(wrk[:, 20:28], iota8[:, :], m[:, 3:4], None, op0=AL.is_equal)
                eng.drain()
                eng.tensor_tensor(wrk[:, 20:28], wrk[:, 20:28], wrk[:, 12:20], op=AL.mult)
                eng.drain()
                eng.reduce_sum(m[:, 5:6], wrk[:, 20:28], axis=AX)
                eng.drain()
                return eng.engine_nop()
            P.add("vector", pref_ev)


        if ci == T // CH2 - 1:
            P.dma("scalar", lambda e, b=b: e.dma_start(out=cc_in_d[b:b + 1, :], in_=carry[0:1, :]))

    if "p1b" in phases:
        prep_attnw(Wo_in, wo_hi, wo_lo, x4_sb, scr_b)
        for b in range(B):
            for ci in range(T // CH2):
                p1b_chunk(b, ci)

        def zrow(eng):
            eng.memset(xl2h[:, :], 0.0)
            eng.memset(xl2l[:, :], 0.0)
            eng.drain()
            return eng.engine_nop()
        P.add("vector", zrow)
        P.dma("scalar", lambda e: e.dma_start(out=xln2_cat_d[DUMMY:DUMMY + 1, 0:D], in_=xl2h[0:1, :]))
        P.dma("scalar", lambda e: e.dma_start(out=xln2_cat_d[DUMMY:DUMMY + 1, D:2 * D], in_=xl2l[0:1, :]))

    def p15():
        def tinit(eng):
            eng.memset(x4_sb[:, 0:161], 0.0)
            eng.drain()
            return eng.engine_nop()
        P.add("vector", tinit)
        P.dma("scalar", lambda e: e.dma_start(
            out=tokplane_d[0:NSLOT_PAD].rearrange("(p a) -> p a", p=128),
            in_=x4_sb[:, 0:161]))
        P.dma("scalar", lambda e: e.dma_start(
            out=gateplane_d[0:NSLOT_PAD].rearrange("(p a) -> p a", p=128),
            in_=x4_sb[:, 0:161]))

        def ag(eng):
            return eng.collective_compute(
                "AllGather", AL.bypass, replica_groups=[list(range(8))],
                ins=[cc_in_d[:, :]], outs=[cc_out_d[:, :]])
        P.add("gpsimd", ag)
        P.dma("sync", lambda e: e.dma_start(out=x4_sb[0:32, 1024:1024 + NE], in_=cc_out_d[:, :]))
        if cc_dbg_d is not None:
            P.dma("gpsimd", lambda e: e.dma_start(out=cc_dbg_d[:, :], in_=x4_sb[0:32, 1024:1024 + NE]))
        P.dma("sync", lambda e: e.dma_start(out=x4_sb[0:32, 1040:1045], in_=maskT_in[:, :]))

        P.add("tensor", lambda e: e.matmul(psum[3][0:5, 0:NE], x4_sb[0:32, 1040:1045],
                                           x4_sb[0:32, 1024:1024 + NE], start=True, stop=True))

        def basev(eng):
            eng.tensor_copy(out=x4_sb[0:5, 2048:2048 + NE], in_=psum[3][0:5, 0:NE])
            eng.drain()
            return eng.engine_nop()
        P.add("vector", basev)
        P.dma("gpsimd", lambda e: e.dma_start(out=x4_sb[0:1, 2560:2560 + 40],
                                              in_=x4_sb[0:5, 2048:2048 + NE]))

        for b_ in range(B):
            P.add("tensor", lambda e, b_=b_: e.matmul(
                psum[3][:, 0:NE], ones128[0:1, :], x4_sb[0:1, 2560 + b_ * NE:2560 + (b_ + 1) * NE],
                start=True, stop=True))
            P.add("vector", lambda e, b_=b_: e.tensor_copy(out=lcb_bc[:, b_, :],
                                                           in_=psum[3][:, 0:NE]))
        P.add("tensor", lambda e: e.matmul(psum[3][:, 0:NE], ones128[0:1, :],
                                           x4_sb[0:1, 2592:2600], start=True, stop=True))

        def crossv(eng):
            eng.tensor_copy(out=wrk[:, 0:8], in_=psum[3][:, 0:NE])
            eng.drain()
            for b_ in range(B):
                eng.tensor_tensor(lcb_bc[:, b_, :], lcb_bc[:, b_, :], wrk[:, 0:8],
                                  op=AL.subtract)
            eng.tensor_scalar(tau_bc[:, :], wrk[:, 0:8], -1.0, float(CAP),
                              op0=AL.mult, op1=AL.add)
            eng.drain()
            return eng.engine_nop()
        P.add("vector", crossv)

        for tile_idx in range(64):
            b_ = tile_idx // 16
            trow = tile_idx * 128

            def slots(eng, tile_idx=tile_idx, b_=b_, trow=trow):
                m = meta_sb[:, tile_idx, :]
                for ch, (ecol, rcol) in enumerate(((2, 4), (3, 5))):
                    w0 = 4 + ch * 12
                    oh = wrk[:, w0:w0 + 8]
                    eng.tensor_scalar(oh, iota8[:, :], m[:, ecol:ecol + 1], None, op0=AL.is_equal)
                    eng.drain()
                    eng.tensor_tensor(oh, oh, lcb_bc[:, b_, :], op=AL.mult)
                    eng.drain()
                    eng.reduce_sum(wrk[:, w0 + 8:w0 + 9], oh, axis=AX)
                    eng.drain()
                    eng.tensor_tensor(wrk[:, w0 + 8:w0 + 9], wrk[:, w0 + 8:w0 + 9],
                                      m[:, rcol:rcol + 1], op=AL.add)
                    eng.drain()
                    eng.tensor_scalar(wrk[:, w0 + 9:w0 + 10], m[:, ecol:ecol + 1],
                                      float(C_LOC), None, op0=AL.mult)
                    eng.drain()
                    eng.tensor_tensor(wrk[:, w0 + 9:w0 + 10], wrk[:, w0 + 9:w0 + 10],
                                      wrk[:, w0 + 8:w0 + 9], op=AL.add)
                    eng.drain()
                    eng.tensor_scalar(wrk[:, w0 + 10:w0 + 11], wrk[:, w0 + 8:w0 + 9],
                                      float(C_LOC), None, op0=AL.is_ge)
                    eng.drain()
                    eng.tensor_scalar(wrk[:, w0 + 11:w0 + 12], wrk[:, w0 + 9:w0 + 10],
                                      -1.0, float(NSLOT_PAD), op0=AL.mult, op1=AL.add)
                    eng.drain()
                    eng.tensor_tensor(wrk[:, w0 + 11:w0 + 12], wrk[:, w0 + 11:w0 + 12],
                                      wrk[:, w0 + 10:w0 + 11], op=AL.mult)
                    eng.drain()
                    eng.tensor_tensor(wrk[:, w0 + 9:w0 + 10], wrk[:, w0 + 9:w0 + 10],
                                      wrk[:, w0 + 11:w0 + 12], op=AL.add)
                    eng.drain()
                eng.tensor_copy(out=wrk_i[:, 0:1], in_=wrk[:, 13:14])
                eng.tensor_copy(out=wrk_i[:, 1:2], in_=wrk[:, 25:26])
                eng.tensor_scalar_add(wrk[:, 0:1], iotap[:, :], float(trow))
                eng.drain()
                return eng.engine_nop()
            P.add("vector", slots)
            for chn in range(2):
                def scat(eng, chn=chn):
                    return eng.indirect_dma_start(
                        out=tokplane_d.rearrange("(a o) -> a o", o=1),
                        out_offset=IndirectOffsetOnAxis(wrk_i[:, chn:chn + 1], 0),
                        in_=wrk[:, 0:1], in_offset=None)
                P.dma("gpsimd", scat)

                def scatg(eng, chn=chn, tile_idx=tile_idx):
                    return eng.indirect_dma_start(
                        out=gateplane_d.rearrange("(a o) -> a o", o=1),
                        out_offset=IndirectOffsetOnAxis(wrk_i[:, chn:chn + 1], 0),
                        in_=meta_sb[:, tile_idx, chn:chn + 1], in_offset=None)
                P.dma("gpsimd", scatg)

        def idxz(eng):
            eng.memset(idx16[:, :], 0)
            eng.drain()
            return eng.engine_nop()
        P.add("vector", idxz)
        for t_ in range(11):
            r0 = t_ * 128
            rows = min(128, 1280 - r0)
            if rows <= 0:
                continue
            P.dma("sync", lambda e, r0=r0, rows=rows: e.dma_start(
                out=x4_sb[0:rows, 0:16],
                in_=tokplane_d[r0 * 16:(r0 + rows) * 16].rearrange("(a b) -> a b", b=16)))
            P.add("tensor", lambda e, rows=rows: e.transpose(
                psum[3][0:16, 0:rows], x4_sb[0:rows, 0:16], ident_f[0:rows, 0:rows]))

            def evt(eng, r0=r0, rows=rows):
                eng.tensor_copy(out=scr_b[0:16, 0:rows], in_=psum[3][0:16, 0:rows])
                eng.drain()
                eng.tensor_copy(out=idx16[0:16, r0:r0 + rows], in_=scr_b[0:16, 0:rows])
                eng.drain()
                return eng.engine_nop()
            P.add("vector", evt)

        P.dma("gpsimd", lambda e: e.dma_start(out=idx16[16:32, :], in_=idx16[0:16, :]))

    if "p15" in phases:
        p15()

    es_b.close()

    es_c = ExitStack()
    NTK = C_LOC // 256
    FCQ = 8
    w1c_hi = sbt(es_c, "w1c_hi", [128, FCQ * D], BF16)
    w1c_lo = sbt(es_c, "w1c_lo", [128, FCQ * D], BF16)
    w2c_hi = sbt(es_c, "w2c_hi", [128, FCQ * D], BF16)
    w2c_lo = sbt(es_c, "w2c_lo", [128, FCQ * D], BF16)
    wstg = sbt(es_c, "wstg", [128, 4, D], F32)
    xg = [sbt(es_c, f"xg{j}", [128, 2 * DC, 512], BF16) for j in range(5)]

    hf_sb = sbt(es_c, "hf_sb", [128, 2, 256], F32)
    hh_sb = sbt(es_c, "hh_sb", [128, 2, 256], BF16)
    hl_sb = sbt(es_c, "hl_sb", [128, 2, 256], BF16)
    g_sb = sbt(es_c, "g_sb", [128, NTK * 2], F32)
    tk_i = sbt(es_c, "tk_i", [128, NTK * 2], I32)
    y_sb = sbt(es_c, "y_sb", [128, 2, D], F32)
    idxw = sbt(es_c, "idxw", [128, C_LOC // 16], I16)
    idxg = sbt(es_c, "idxg", [128, 32], I16)
    pl_sb = sbt(es_c, "pl_sb", [128, 128], F32)

    def p2_expert(e_, p2_state):
        c0 = (e_ * C_LOC) // 16
        PP = P2PARTS

        def idxcp(eng):
            eng.tensor_copy(out=idxw[:, :], in_=idx16[:, c0:c0 + C_LOC // 16])
            eng.drain()
            return eng.engine_nop()
        P.add("vector", idxcp)
        if "g" not in PP:
            return
        for j in range(5):
            def idxcp2(eng, j=j):
                eng.tensor_copy(out=idxg[:, :], in_=idxw[:, j * 32:(j + 1) * 32])
                eng.drain()
                return eng.engine_nop()
            P.add("vector", idxcp2)
            P.dma("gpsimd", lambda e, j=j: e.dma_gather(
                xg[j][:, :, :], xln2_cat_d[:, :], idxg[:, :],
                num_idxs=512, num_idxs_reg=512, elem_size=2 * D, transpose=True,
                single_packet=False))

        if "c" not in PP:
            return

        P.dma("sync", lambda e, e_=e_: e.dma_start(
            out=pl_sb[0:20, 0:128],
            in_=tokplane_d[e_ * C_LOC:(e_ + 1) * C_LOC].rearrange("(a b) -> a b", b=128)))
        P.dma("sync", lambda e, e_=e_: e.dma_start(
            out=pl_sb[20:40, 0:128],
            in_=gateplane_d[e_ * C_LOC:(e_ + 1) * C_LOC].rearrange("(a b) -> a b", b=128)))
        P.add("tensor", lambda e: e.transpose(psum[3][:, 0:40], pl_sb[0:40, 0:128],
                                              ident_f[0:40, 0:40]))

        def gates(eng, e_=e_):
            eng.tensor_copy(out=wrk[:, 0:20], in_=psum[3][:, 0:20])
            eng.drain()
            eng.tensor_copy(out=tk_i[:, :], in_=wrk[:, 0:20])
            eng.tensor_copy(out=g_sb[:, :], in_=psum[3][:, 20:40])
            eng.drain()
            for sub in range(NTK * 2):
                eng.tensor_scalar_add(wrk[:, 20:21], iotap[:, :], float(sub * 128))
                eng.drain()
                eng.tensor_tensor(wrk[:, 20:21], wrk[:, 20:21], tau_bc[:, e_:e_ + 1],
                                  op=AL.is_lt)
                eng.drain()
                eng.tensor_tensor(g_sb[:, sub:sub + 1], g_sb[:, sub:sub + 1],
                                  wrk[:, 20:21], op=AL.mult)
                eng.drain()
            return eng.engine_nop()
        P.add("vector", gates)

        if "w" not in PP:
            return
        st = p2_state
        for q in range(4):
            qbar = P.add("tensor", lambda e: e.matmul(psum[3][0:1, 0:1], ones128[0:1, 0:1], ones128[0:1, 0:1], start=True, stop=True))
            w1hi_v = {}; w1lo_v = {}; w2hi_v = {}; w2lo_v = {}
            ld1_v = {}; ld2_v = {}
            for fc in range(FCQ):
                fa = q * (FFN // 4) + fc * 128
                sg1 = fc % 2
                sg2 = 2 + fc % 2
                _ldfn1 = lambda e, e_=e_, fa=fa, sg1=sg1: e.dma_start(
                    out=wstg[:, sg1, :].rearrange("p (c f) -> p c f", c=DC),
                    in_=w1_in[e_].rearrange("(c p) f -> p c f", p=128)[:, :, fa:fa + 128])
                if "c" in P2RELAX:
                    ld1_v[fc] = P.dmam("sync", _ldfn1,
                        [("tensor", qbar), ("scalar", w1hi_v.get(fc - 2)),
                         ("vector", w1lo_v.get(fc - 2))], ctr=f"ldw1{fc % 2}")
                else:
                    ld1_v[fc] = P.dma("sync", _ldfn1)
                _hifn1 = lambda e, fc=fc, sg1=sg1: e.copy(
                    out=w1c_hi[:, fc * D:(fc + 1) * D], in_=wstg[:, sg1, :])
                if "c" in P2RELAX:
                    w1hi_v[fc] = P.addm("scalar", _hifn1, [(f"ldw1{fc % 2}", ld1_v[fc])])
                else:
                    w1hi_v[fc] = P.add("scalar", _hifn1)

                def wlo1(eng, fc=fc, sg1=sg1):
                    eng.tensor_tensor(out=w1c_lo[:, fc * D:(fc + 1) * D], in0=wstg[:, sg1, :],
                                      in1=w1c_hi[:, fc * D:(fc + 1) * D], op=AL.subtract)
                    eng.drain()
                    return eng.engine_nop()
                if "c" in P2RELAX:
                    w1lo_v[fc] = P.addm("vector", wlo1, [("scalar", w1hi_v[fc])])
                else:
                    w1lo_v[fc] = P.add("vector", wlo1)

                _ldfn2 = lambda e, e_=e_, fa=fa, sg2=sg2: e.dma_start(
                    out=wstg[:, sg2, :], in_=w2_in[e_, fa:fa + 128, :])
                if "c" in P2RELAX:
                    ld2_v[fc] = P.dmam("sync", _ldfn2,
                        [("tensor", qbar), ("scalar", w2hi_v.get(fc - 2)),
                         ("vector", w2lo_v.get(fc - 2))], ctr=f"ldw2{fc % 2}")
                else:
                    ld2_v[fc] = P.dma("sync", _ldfn2)
                _hifn2 = lambda e, fc=fc, sg2=sg2: e.copy(
                    out=w2c_hi[:, fc * D:(fc + 1) * D], in_=wstg[:, sg2, :])
                if "c" in P2RELAX:
                    w2hi_v[fc] = P.addm("scalar", _hifn2, [(f"ldw2{fc % 2}", ld2_v[fc])])
                else:
                    w2hi_v[fc] = P.add("scalar", _hifn2)

                def wlo2(eng, fc=fc, sg2=sg2):
                    eng.tensor_tensor(out=w2c_lo[:, fc * D:(fc + 1) * D], in0=wstg[:, sg2, :],
                                      in1=w2c_hi[:, fc * D:(fc + 1) * D], op=AL.subtract)
                    eng.drain()
                    return eng.engine_nop()
                if "c" in P2RELAX:
                    w2lo_v[fc] = P.addm("vector", wlo2, [("scalar", w2hi_v[fc])])
                else:
                    w2lo_v[fc] = P.add("vector", wlo2)

            if "h" not in PP:
                continue
            for k in range(NTK):
                def mk_hmm(fc, k):
                    def hmm(eng):
                        bank = psum[fc % 2]
                        ins = None
                        for dc in range(DC):
                            lh = w1c_hi[:, fc * D + dc * 128:fc * D + (dc + 1) * 128]
                            ll = w1c_lo[:, fc * D + dc * 128:fc * D + (dc + 1) * 128]
                            rh = xg[k // 2][:, dc, (k % 2) * 256:(k % 2) * 256 + 256]
                            rl = xg[k // 2][:, DC + dc, (k % 2) * 256:(k % 2) * 256 + 256]
                            eng.matmul(bank[:, 0:256], lh, rh, start=(dc == 0), stop=False)
                            eng.matmul(bank[:, 0:256], lh, rl, start=False, stop=False)
                            ins = eng.matmul(bank[:, 0:256], ll, rh, start=False,
                                             stop=(dc == DC - 1))
                        return ins
                    return hmm

                def mk_gelu(fc, k):
                    def gelu(eng):
                        b = fc % 2
                        eng.activation(out=hf_sb[:, b, :], in_=psum[fc % 2][:, 0:256],
                                       func=AF.Gelu_apprx_tanh)
                        eng.drain()
                        return eng.copy(out=hh_sb[:, b, :], in_=hf_sb[:, b, :])
                    return gelu

                def mk_losub(fc, k):
                    def losub(eng):
                        b = fc % 2
                        eng.tensor_tensor(out=hl_sb[:, b, :], in0=hf_sb[:, b, :],
                                          in1=hh_sb[:, b, :], op=AL.subtract)
                        eng.drain()
                        return eng.engine_nop()
                    return losub

                def mk_ymm(fc, k):
                    def ymm(eng):
                        b = fc % 2
                        ins = None
                        for sub in range(2):
                            for dh in range(2):
                                bank = psum[4 + sub * 2 + dh]
                                lh = hh_sb[:, b, sub * 128:(sub + 1) * 128]
                                ll = hl_sb[:, b, sub * 128:(sub + 1) * 128]
                                rh = w2c_hi[:, fc * D + dh * 512:fc * D + (dh + 1) * 512]
                                rl = w2c_lo[:, fc * D + dh * 512:fc * D + (dh + 1) * 512]
                                eng.matmul(bank[:, :], lh, rh, start=(fc == 0), stop=False)
                                eng.matmul(bank[:, :], lh, rl, start=False, stop=False)
                                ins = eng.matmul(bank[:, :], ll, rh, start=False,
                                                 stop=(fc == FCQ - 1))
                        return ins
                    return ymm

                for fc in range(FCQ):
                    if "h" in P2RELAX:
                        st["hmm"][(k, fc)] = P.addm("tensor", mk_hmm(fc, k),
                            [("scalar", w1hi_v[fc]), ("vector", w1lo_v[fc]),
                             ("scalar", w2hi_v[fc]), ("vector", w2lo_v[fc]),
                             ("scalar", st["gelu"].get((k, fc - 2)) or st["gelu"].get((k - 1, FCQ + fc - 2)))])
                    else:
                        st["hmm"][(k, fc)] = P.add("tensor", mk_hmm(fc, k))
                    if fc >= 1 and "y" in PP:
                        if "y" in P2RELAX:
                            st["ymm"][(k, fc - 1)] = P.addm("tensor", mk_ymm(fc - 1, k),
                                [("vector", st["losub"][(k, fc - 1)]),
                                 ("vector", st["yev"].get(k - 1) if fc - 1 == 0 else None)])
                        else:
                            st["ymm"][(k, fc - 1)] = P.add("tensor", mk_ymm(fc - 1, k))
                    if "g" in P2RELAX:
                        st["gelu"][(k, fc)] = P.addm("scalar", mk_gelu(fc, k),
                            [("tensor", st["hmm"][(k, fc)]),
                             ("vector", st["losub"].get((k, fc - 2)) or st["losub"].get((k - 1, FCQ + fc - 2))),
                             ("tensor", st["ymm"].get((k, fc - 2)) or st["ymm"].get((k - 1, FCQ + fc - 2)))])
                        st["losub"][(k, fc)] = P.addm("vector", mk_losub(fc, k),
                            [("scalar", st["gelu"][(k, fc)]),
                             ("tensor", st["ymm"].get((k, fc - 2)) or st["ymm"].get((k - 1, FCQ + fc - 2)))])
                    else:
                        st["gelu"][(k, fc)] = P.add("scalar", mk_gelu(fc, k))
                        st["losub"][(k, fc)] = P.add("vector", mk_losub(fc, k))
                if "y" not in PP:
                    continue
                if "y" in P2RELAX:
                    st["ymm"][(k, FCQ - 1)] = P.addm("tensor", mk_ymm(FCQ - 1, k),
                        [("vector", st["losub"][(k, FCQ - 1)])])
                else:
                    st["ymm"][(k, FCQ - 1)] = P.add("tensor", mk_ymm(FCQ - 1, k))

                for sub in range(2):
                    def yev(eng, sub=sub, k=k):
                        for dh in range(2):
                            eng.tensor_scalar_mul(y_sb[:, sub, dh * 512:(dh + 1) * 512],
                                                  psum[4 + sub * 2 + dh][:, :],
                                                  g_sb[:, 2 * k + sub:2 * k + sub + 1])
                        eng.drain()
                        return eng.engine_nop()
                    if "y" in P2RELAX:
                        yv = P.addm("vector", yev,
                                    [("tensor", st["ymm"][(k, FCQ - 1)]),
                                     ("gpsimd", st["yscat"].get((k - 1, sub)))])
                    else:
                        yv = P.add("vector", yev)
                    if sub == 1:
                        st["yev"][k] = yv
                    if "s" not in PP:
                        continue

                    def yscat(eng, sub=sub, k=k):
                        return eng.indirect_dma_start(
                            out=out[:, :],
                            out_offset=IndirectOffsetOnAxis(tk_i[:, 2 * k + sub:2 * k + sub + 1], 0),
                            in_=y_sb[:, sub, :], in_offset=None, compute_op=AL.add)
                    if "y" in P2RELAX:
                        st["yscat"][(k, sub)] = P.dmam("gpsimd", yscat,
                            [("vector", yv), ("gpsimd", st["lastscat"][0])])
                    else:
                        st["yscat"][(k, sub)] = P.dma("gpsimd", yscat)
                    st["lastscat"][0] = st["yscat"][(k, sub)]

    if "p2" in phases:
        P.add("gpsimd", lambda e: (e.load_library(LIB_MLP), e.engine_nop())[-1])
        for nm in ("ldw10", "ldw11", "ldw20", "ldw21"):
            P.new_ctr(nm)
        p2_state = {"hmm": {}, "gelu": {}, "losub": {}, "ymm": {}, "yev": {},
                    "yscat": {}, "lastscat": [None]}
        for e_ in range(min(NE, P2N)):
            p2_expert(e_, p2_state)

    es_c.close()

    P.add("gpsimd", lambda e: e.engine_nop())

    block = es.enter_context(nc.Block())
    P.emit(block)
    es.close()
    mybir.codegen_inst_isa_subclasses(nc)
    return nc


def make_inputs(inputs, core):
    vv = np.stack([np.asarray(inputs[k], np.float32) for k in
                   ("L0_attn_u", "L0_attn_w", "L0_attn_tmk", "L0_attn_tmv", "L0_attn_tmr",
                    "L0_attn_bo")])
    mask = np.zeros((32, 5), np.float32)
    for bl in range(4):
        mask[: 4 * core + bl, bl] = 1.0
    mask[: 4 * core, 4] = 1.0
    return {
        "x": np.ascontiguousarray(np.asarray(inputs["L0_attn_x"], np.float32)[4 * core:4 * core + 4]),
        "Wk": np.asarray(inputs["L0_attn_Wk"], np.float32),
        "Wv": np.asarray(inputs["L0_attn_Wv"], np.float32),
        "Wr": np.asarray(inputs["L0_attn_Wr"], np.float32),
        "Wo": np.asarray(inputs["L0_attn_Wo"], np.float32),
        "vecs": vv,
        "w_router": np.asarray(inputs["L0_ffn_w_router"], np.float32),
        "w1": np.asarray(inputs["L0_ffn_w1"], np.float32),
        "w2": np.asarray(inputs["L0_ffn_w2"], np.float32),
        "bias": np.asarray(inputs["L0_ffn_bias"], np.float32).reshape(1, D),
        "maskT": mask,
    }


def kernel(**inputs):
    nc = build_program()
    in_maps = [make_inputs(inputs, core) for core in range(8)]
    res = run_bass_kernel_spmd(nc, in_maps, core_ids=list(range(8)))
    shards = [res.results[cu]["out"].reshape(B, T, D) for cu in range(8)]
    return np.concatenate(shards, axis=0)


# version 03
# speedup vs baseline: 1.5382x; 1.0370x over previous
import numpy as np
from contextlib import ExitStack

import concourse.bass as bass
import concourse.mybir as mybir
from concourse.bass_utils import run_bass_kernel_spmd
from concourse.bass_core_rust import IndirectOffsetOnAxis
from concourse.library_config import mlp as LIB_MLP

F32 = mybir.dt.float32
BF16 = mybir.dt.bfloat16
I16 = mybir.dt.int16
I32 = mybir.dt.int32
AF = mybir.ActivationFunctionType
AL = mybir.AluOpType
AX = mybir.AxisListType.X

D = 1024
T = 2048
B = 4
DC = 8
FFN = 4096
NE = 8
C_LOC = 2560
NSLOT = NE * C_LOC
NSLOT_PAD = 20608
CAP = 16384
CH1 = 256
CH2 = 512
DUMMY = B * T
P2N = 8
P2RELAX = "chgy"
P2PARTS = "gwchys"
ENGS = ("sync", "scalar", "vector", "tensor", "gpsimd")


class Ctr:
    def __init__(self, nc, es, name):
        self.sem = es.enter_context(nc.semaphore(name))
        self.name = name
        self.n = 0


class Prog:

    def __init__(self, nc, es):
        self.nc = nc
        self.es = es
        self.ctr = {e: Ctr(nc, es, f"c_{e}") for e in ENGS}
        self.steps = []
        self.seen = {e: {e2: 0 for e2 in ENGS} for e in ENGS}

    def new_ctr(self, name):
        self.ctr[name] = Ctr(self.nc, self.es, name)
        for e in ENGS:
            self.seen[e][name] = 0
        return name

    def add(self, eng, fn, inck=1):
        waits = []
        for e2 in self.ctr:
            v = self.ctr[e2].n
            if v > self.seen[eng].get(e2, 0):
                waits.append((self.ctr[e2].sem, v))
                self.seen[eng][e2] = v
        self.steps.append((eng, waits, fn, (self.ctr[eng].sem, inck)))
        self.ctr[eng].n += inck
        return self.ctr[eng].n

    def addm(self, eng, fn, mwaits, inck=1, ctr=None):
        waits = []
        for (en, v) in mwaits:
            if v is None:
                continue
            if v > self.seen[eng].get(en, 0):
                waits.append((self.ctr[en].sem, v))
                self.seen[eng][en] = v
        tgt = ctr if ctr is not None else eng
        self.steps.append((eng, waits, fn, (self.ctr[tgt].sem, inck)))
        self.ctr[tgt].n += inck
        return self.ctr[tgt].n

    def dma(self, eng, fn):
        return self.add(eng, fn, inck=16)

    def dmam(self, eng, fn, mwaits, ctr=None):
        return self.addm(eng, fn, mwaits, inck=16, ctr=ctr)

    def emit(self, block):
        for eng_name in ENGS:
            steps = [s for s in self.steps if s[0] == eng_name]

            def body(eng, steps=steps):
                for (_e, waits, fn, inc) in steps:
                    for (sem, v) in waits:
                        eng.wait_ge(sem, v)
                    ins = fn(eng)
                    if isinstance(ins, (list, tuple)):
                        ins = ins[-1]
                    assert ins is not None
                    ins.then_inc(inc[0], inc[1])
            getattr(block, eng_name)(body)


def build_program(phases=("p0", "p1a", "p1b", "p15", "p2"), debug=()):
    nc = bass.Bass("TRN2", target_bir_lowering=False, debug=False)
    es = ExitStack()

    def din(name, shape, dt=F32):
        return nc.dram_tensor(name, shape, dt, kind="ExternalInput").ap()

    x_in = din("x", [B, T, D])
    Wk_in = din("Wk", [D, D]); Wv_in = din("Wv", [D, D])
    Wr_in = din("Wr", [D, D]); Wo_in = din("Wo", [D, D])
    vecs_in = din("vecs", [6, D])
    wr_in = din("w_router", [D, NE])
    w1_in = din("w1", [NE, D, FFN])
    w2_in = din("w2", [NE, FFN, D])
    bias_in = din("bias", [1, D])
    maskT_in = din("maskT", [32, 5])

    out = nc.dram_tensor("out", [B * T, D], F32, kind="ExternalOutput").ap()

    def dscr(name, shape, dt):
        if name in debug:
            return nc.dram_tensor(name, shape, dt, kind="ExternalOutput").ap()
        return nc.dram_tensor(name, shape, dt).ap()

    rwkv_hi_d = dscr("rwkv_hi", [D, B * T], BF16)
    rwkv_lo_d = dscr("rwkv_lo", [D, B * T], BF16)
    xln2_cat_d = dscr("xln2_cat", [B * T + 1, 2 * D], BF16)
    tokplane_d = dscr("tokplane", [NSLOT_PAD + 1], F32)
    gateplane_d = dscr("gateplane", [NSLOT_PAD + 1], F32)
    cc_in_d = nc.dram_tensor("cc_in", [B, NE], F32).ap()
    cc_out_d = nc.dram_tensor("cc_out", [32, NE], F32).ap()
    cc_dbg_d = (nc.dram_tensor("cc_dbg", [32, NE], F32, kind="ExternalOutput").ap()
                if "cc_out" in debug else None)

    def sbt(stack, name, shape, dt):
        return stack.enter_context(nc.sbuf_tensor(name, shape, dt))

    ident_f = sbt(es, "ident_f", [128, 128], F32)
    ones128 = sbt(es, "ones128", [128, 128], F32)
    Lstr_f = sbt(es, "Lstr_f", [128, 128], F32)
    iota8 = sbt(es, "iota8", [128, NE], F32)
    iotap = sbt(es, "iotap", [128, 1], F32)
    vecs = sbt(es, "vecs_sb", [128, 6, DC], F32)
    omt = sbt(es, "omt", [128, 3, DC], F32)
    negw = sbt(es, "negw", [128, DC], F32)
    wrh = sbt(es, "wrh", [128, DC, NE], BF16)
    wrl = sbt(es, "wrl", [128, DC, NE], BF16)
    cs_bc = sbt(es, "cs_bc", [128, NE], F32)
    bias_bc = sbt(es, "bias_bc", [128, D], F32)
    stat = sbt(es, "stat", [128, 16], F32)
    wrk = sbt(es, "wrk", [128, 32], F32)
    wrk_i = sbt(es, "wrk_i", [128, 4], I32)
    meta_sb = sbt(es, "meta_sb", [128, 64, 6], F32)
    carry = sbt(es, "carry", [128, NE], F32)
    idx16 = sbt(es, "idx16", [128, 1280], I16)
    tau_bc = sbt(es, "tau_bc", [128, NE], F32)
    lcb_bc = sbt(es, "lcb_bc", [128, B, NE], F32)

    psum = [es.enter_context(nc.psum_tensor(f"ps{i}", [128, 512], F32)) for i in range(8)]

    P = Prog(nc, es)

    es_a = ExitStack()
    wk_hi = sbt(es_a, "wk_hi", [128, DC * D], BF16); wk_lo = sbt(es_a, "wk_lo", [128, DC * D], BF16)
    wv_hi = sbt(es_a, "wv_hi", [128, DC * D], BF16); wv_lo = sbt(es_a, "wv_lo", [128, DC * D], BF16)
    wq_hi = sbt(es_a, "wq_hi", [128, DC * D], BF16); wq_lo = sbt(es_a, "wq_lo", [128, DC * D], BF16)
    x_tok = sbt(es_a, "x_tok", [128, 2 * D], F32)
    sq_scr = sbt(es_a, "sq_scr", [128, D], BF16)
    xln_fm = sbt(es_a, "xln_fm", [128, DC * (CH1 + 1)], F32)
    mix_f = sbt(es_a, "mix_f", [128, DC * CH1], F32)
    mixh = sbt(es_a, "mixh", [128, DC * CH1], BF16)
    mixl = sbt(es_a, "mixl", [128, DC * CH1], BF16)
    k_sb = sbt(es_a, "k_sb", [128, DC * CH1], F32)
    v_sb = sbt(es_a, "v_sb", [128, DC * CH1], F32)
    r_sb = sbt(es_a, "r_sb", [128, DC * CH1], F32)
    e1_sb = sbt(es_a, "e1_sb", [128, DC * CH1], F32)
    e2_sb = sbt(es_a, "e2_sb", [128, DC * CH1], F32)
    a_sb = sbt(es_a, "a_sb", [128, DC * CH1], F32)
    b_sb = sbt(es_a, "b_sb", [128, DC * CH1], F32)
    st_a = sbt(es_a, "st_a", [128, DC], F32)
    st_b = sbt(es_a, "st_b", [128, DC], F32)
    mixf_b = mix_f[:, :].bitcast(BF16)
    rwh_sb = mixf_b[:, 0:DC * CH1]
    rwl_sb = mixf_b[:, DC * CH1:2 * DC * CH1]

    def p0_consts(eng):
        eng.memset(ident_f[:, :], 1.0)
        eng.memset(ones128[:, :], 1.0)
        eng.memset(Lstr_f[:, :], 1.0)
        eng.drain()
        eng.affine_select(ident_f[:, :], ident_f[:, :], pattern=[[-1, 128]],
                          compare_op=AL.is_equal, fill=0.0, base=0, channel_multiplier=1)
        eng.affine_select(Lstr_f[:, :], Lstr_f[:, :], pattern=[[1, 128]],
                          compare_op=AL.is_gt, fill=0.0, base=0, channel_multiplier=-1)
        eng.iota(iota8[:, :], pattern=[[1, NE]], base=0, channel_multiplier=0,
                 allow_small_or_imprecise_dtypes=True)
        return eng.iota(iotap[:, :], pattern=[[1, 1]], base=0, channel_multiplier=1,
                        allow_small_or_imprecise_dtypes=True)
    P.add("gpsimd", p0_consts)

    P.dma("sync", lambda e: e.dma_start(
        out=x_tok[0:48, 0:128],
        in_=vecs_in.rearrange("a (c p) -> (a c) p", p=128)))
    P.add("tensor", lambda e: e.transpose(psum[0][:, 0:48], x_tok[0:48, 0:128], ident_f[0:48, 0:48]))

    def p0_vecs(eng):
        eng.tensor_copy(out=vecs[:, :, :].rearrange("p j c -> p (j c)"), in_=psum[0][:, 0:48])
        eng.drain()
        for j in range(3):
            eng.tensor_scalar(omt[:, j, :], vecs[:, 2 + j, :], -1.0, 1.0,
                              op0=AL.mult, op1=AL.add)
        eng.tensor_scalar_mul(negw[:, :], vecs[:, 1, :], -1.0)
        eng.drain()
        return eng.engine_nop()
    P.add("vector", p0_vecs)

    P.dma("sync", lambda e: e.dma_start(
        out=mix_f[:, 0:DC * NE].rearrange("p (c e) -> p c e", c=DC),
        in_=wr_in.rearrange("(c p) e -> p c e", p=128)))

    def p0_wr(eng):
        eng.tensor_copy(out=wrh[:, :, :].rearrange("p c e -> p (c e)"), in_=mix_f[:, 0:DC * NE])
        eng.drain()
        eng.tensor_tensor(out=wrl[:, :, :].rearrange("p c e -> p (c e)"),
                          in0=mix_f[:, 0:DC * NE],
                          in1=wrh[:, :, :].rearrange("p c e -> p (c e)"), op=AL.subtract)
        eng.drain()
        return eng.engine_nop()
    P.add("vector", p0_wr)

    def p0_cs(eng):
        ins = None
        for dcq in range(DC):
            ins = eng.matmul(psum[1][0:1, 0:NE], ones128[:, 0:1],
                             mix_f[:, dcq * NE:(dcq + 1) * NE],
                             start=(dcq == 0), stop=(dcq == DC - 1))
        return ins
    P.add("tensor", p0_cs)
    P.add("vector", lambda e: e.tensor_copy(out=stat[0:1, 0:NE], in_=psum[1][0:1, 0:NE]))
    P.add("tensor", lambda e: e.matmul(psum[2][:, 0:NE], ones128[0:1, :], stat[0:1, 0:NE],
                                       start=True, stop=True))
    P.add("vector", lambda e: e.tensor_copy(out=cs_bc[:, :], in_=psum[2][:, 0:NE]))

    P.dma("sync", lambda e: e.dma_start(out=x_tok[0:1, 0:D], in_=bias_in[:, :]))

    def p0_bias(eng):
        eng.matmul(psum[1][:, :], ones128[0:1, :], x_tok[0:1, 0:512], start=True, stop=True)
        return eng.matmul(psum[2][:, :], ones128[0:1, :], x_tok[0:1, 512:1024],
                          start=True, stop=True)
    P.add("tensor", p0_bias)

    def p0_bias_ev(eng):
        eng.tensor_copy(out=bias_bc[:, 0:512], in_=psum[1][:, :])
        eng.tensor_copy(out=bias_bc[:, 512:1024], in_=psum[2][:, :])
        eng.drain()
        return eng.engine_nop()
    P.add("vector", p0_bias_ev)

    def prep_attnw(W_in, w_hi, w_lo, stg, scr):
        for rc in range(DC):
            P.dma("sync", lambda e, rc=rc: e.dma_start(out=stg[:, 0:D],
                                                       in_=W_in[rc * 128:(rc + 1) * 128, :]))
            for half in range(2):
                def tr(eng, half=half):
                    ins = None
                    for j in range(4):
                        ccd = half * 4 + j
                        ins = eng.transpose(psum[4 + j][:, 0:128],
                                            stg[:, ccd * 128:(ccd + 1) * 128], ident_f[:, :])
                    return ins
                P.add("tensor", tr)

                def ev(eng, rc=rc, half=half, w_hi=w_hi, w_lo=w_lo):
                    for j in range(4):
                        eng.tensor_copy(out=scr[:, j * 128:(j + 1) * 128], in_=psum[4 + j][:, 0:128])
                    eng.drain()
                    for j in range(4):
                        o = (half * 4 + j) * D + rc * 128
                        eng.tensor_copy(out=w_hi[:, o:o + 128], in_=scr[:, j * 128:(j + 1) * 128])
                    eng.drain()
                    for j in range(4):
                        o = (half * 4 + j) * D + rc * 128
                        eng.tensor_tensor(out=w_lo[:, o:o + 128],
                                          in0=scr[:, j * 128:(j + 1) * 128],
                                          in1=w_hi[:, o:o + 128], op=AL.subtract)
                    eng.drain()
                    return eng.engine_nop()
                P.add("vector", ev)

    if "p1a" in phases:
        prep_attnw(Wk_in, wk_hi, wk_lo, k_sb, v_sb)
        prep_attnw(Wv_in, wv_hi, wv_lo, k_sb, v_sb)
        prep_attnw(Wr_in, wq_hi, wq_lo, k_sb, v_sb)

    def p1a_chunk(b, ci):
        t0 = ci * CH1
        first = (ci == 0)
        P.dma("sync", lambda e: e.dma_start(out=x_tok[:, 0:D], in_=x_in[b, t0:t0 + 128, :]))
        P.dma("sync", lambda e: e.dma_start(out=x_tok[:, D:2 * D],
                                            in_=x_in[b, t0 + 128:t0 + 256, :]))

        def ln_sums(eng):
            for ti in range(2):
                eng.reduce_sum(stat[:, ti * 8:ti * 8 + 1], x_tok[:, ti * D:(ti + 1) * D], axis=AX)
            eng.drain()
            return eng.engine_nop()
        P.add("vector", ln_sums)

        def ln_sq(eng):
            ins = None
            for ti in range(2):
                if ti:
                    eng.drain()
                ins = eng.activation(out=sq_scr[:, :], in_=x_tok[:, ti * D:(ti + 1) * D],
                                     func=AF.Square, accum_out=stat[:, ti * 8 + 2:ti * 8 + 3])
            return ins
        P.add("scalar", ln_sq)

        def ln_var(eng):
            for ti in range(2):
                s = stat[:, ti * 8:(ti + 1) * 8]
                eng.tensor_scalar_mul(s[:, 1:2], s[:, 0:1], 1.0 / D)
                eng.tensor_scalar_mul(s[:, 3:4], s[:, 2:3], 1.0 / D)
            eng.drain()
            for ti in range(2):
                s = stat[:, ti * 8:(ti + 1) * 8]
                eng.tensor_tensor(s[:, 4:5], s[:, 1:2], s[:, 1:2], op=AL.mult)
            eng.drain()
            for ti in range(2):
                s = stat[:, ti * 8:(ti + 1) * 8]
                eng.tensor_tensor(s[:, 5:6], s[:, 3:4], s[:, 4:5], op=AL.subtract)
            eng.drain()
            for ti in range(2):
                s = stat[:, ti * 8:(ti + 1) * 8]
                eng.tensor_scalar_add(s[:, 5:6], s[:, 5:6], 1e-5)
            eng.drain()
            return eng.engine_nop()
        P.add("vector", ln_var)

        def ln_sqrt(eng):
            ins = None
            for ti in range(2):
                s = stat[:, ti * 8:(ti + 1) * 8]
                ins = eng.activation(out=s[:, 6:7], in_=s[:, 5:6], func=AF.Sqrt)
            return ins
        P.add("scalar", ln_sqrt)

        def ln_norm(eng):
            for ti in range(2):
                s = stat[:, ti * 8:(ti + 1) * 8]
                eng.reciprocal(s[:, 7:8], s[:, 6:7])
            eng.drain()
            for ti in range(2):
                s = stat[:, ti * 8:(ti + 1) * 8]
                eng.tensor_scalar(x_tok[:, ti * D:(ti + 1) * D], x_tok[:, ti * D:(ti + 1) * D],
                                  s[:, 1:2], s[:, 7:8], op0=AL.subtract, op1=AL.mult)
            eng.drain()
            return eng.engine_nop()
        P.add("vector", ln_norm)

        def shift_prep(eng):
            for dc in range(DC):
                o = dc * (CH1 + 1)
                if first:
                    eng.memset(xln_fm[:, o:o + 1], 0.0)
                else:
                    eng.tensor_copy(out=xln_fm[:, o:o + 1], in_=xln_fm[:, o + CH1:o + CH1 + 1])
            eng.drain()
            return eng.engine_nop()
        P.add("vector", shift_prep)

        for ti in range(2):
            for g in range(2):
                def tr(eng, ti=ti, g=g):
                    ins = None
                    for j in range(4):
                        dc = g * 4 + j
                        ins = eng.transpose(psum[4 + j][:, 0:128],
                                            x_tok[:, ti * D + dc * 128:ti * D + (dc + 1) * 128],
                                            ident_f[:, :])
                    return ins
                P.add("tensor", tr)

                def ev(eng, ti=ti, g=g):
                    for j in range(4):
                        dc = g * 4 + j
                        o = dc * (CH1 + 1) + 1 + ti * 128
                        eng.tensor_copy(out=xln_fm[:, o:o + 128], in_=psum[4 + j][:, 0:128])
                    eng.drain()
                    return eng.engine_nop()
                P.add("vector", ev)

        projs = ((k_sb, wk_hi, wk_lo, 2), (v_sb, wv_hi, wv_lo, 3), (r_sb, wq_hi, wq_lo, 4))
        mixh_b = [mixh, mixl]
        hiv = {}; lov = {}; pjv = {}; evv = {}
        mixv = {}
        for ti_, (dst, w_hi, w_lo, vj) in enumerate(projs):
            def mix(eng, vj=vj):
                for dc in range(DC):
                    o = dc * (CH1 + 1)
                    eng.tensor_scalar_mul(mix_f[:, dc * CH1:(dc + 1) * CH1],
                                          xln_fm[:, o:o + CH1], omt[:, vj - 2, dc:dc + 1])
                eng.drain()
                for dc in range(DC):
                    o = dc * (CH1 + 1)
                    eng.scalar_tensor_tensor(mix_f[:, dc * CH1:(dc + 1) * CH1],
                                             xln_fm[:, o + 1:o + 1 + CH1],
                                             vecs[:, vj, dc:dc + 1],
                                             mix_f[:, dc * CH1:(dc + 1) * CH1],
                                             op0=AL.mult, op1=AL.add)
                eng.drain()
                return eng.engine_nop()
            if ti_ == 0:
                mixv[ti_] = P.add("vector", mix)
            else:
                mixv[ti_] = P.addm("vector", mix, [("scalar", hiv[ti_ - 1])])
            if vj == 4:
                sg = P.addm("scalar", lambda e: e.activation(out=mix_f[:, :], in_=mix_f[:, :],
                                                             func=AF.Sigmoid),
                            [("vector", mixv[ti_])])
            hiv[ti_] = P.addm("scalar", lambda e: e.copy(out=mixh[:, :], in_=mix_f[:, :]),
                              [("vector", mixv[ti_])] +
                              ([("tensor", pjv.get((ti_ - 1, DC - 1)))] if ti_ else []))

            def lo(eng):
                eng.tensor_tensor(out=mixl[:, :], in0=mix_f[:, :], in1=mixh[:, :], op=AL.subtract)
                eng.drain()
                return eng.engine_nop()
            lov[ti_] = P.addm("vector", lo, [("scalar", hiv[ti_])] +
                              ([("tensor", pjv.get((ti_ - 1, DC - 1)))] if ti_ else []))

            for ec in range(DC):
                def pj(eng, ec=ec, w_hi=w_hi, w_lo=w_lo):
                    bank = psum[ec % 2]
                    ins = None
                    for dc in range(DC):
                        lh = w_hi[:, dc * D + ec * 128:dc * D + ec * 128 + 128]
                        ll = w_lo[:, dc * D + ec * 128:dc * D + ec * 128 + 128]
                        rh = mixh[:, dc * CH1:(dc + 1) * CH1]
                        rl = mixl[:, dc * CH1:(dc + 1) * CH1]
                        eng.matmul(bank[:, 0:CH1], lh, rh, start=(dc == 0), stop=False)
                        eng.matmul(bank[:, 0:CH1], lh, rl, start=False, stop=False)
                        ins = eng.matmul(bank[:, 0:CH1], ll, rh, start=False, stop=(dc == DC - 1))
                    return ins
                pjv[(ti_, ec)] = P.addm("tensor", pj,
                    [("scalar", hiv[ti_]), ("vector", lov[ti_]),
                     ("vector", evv.get((ti_, ec - 2)) or (evv.get((ti_ - 1, DC + ec - 2)) if ti_ else None))])
                evv[(ti_, ec)] = P.addm("vector", lambda e, ec=ec, dst=dst: e.tensor_copy(
                    out=dst[:, ec * CH1:(ec + 1) * CH1], in_=psum[ec % 2][:, 0:CH1]),
                    [("tensor", pjv[(ti_, ec)])])

        def sprep(eng):
            for dc in range(DC):
                sl = k_sb[:, dc * CH1:(dc + 1) * CH1]
                eng.tensor_scalar_add(sl, sl, vecs[:, 0, dc:dc + 1])
            eng.drain()
            for dc in range(DC):
                sl = k_sb[:, dc * CH1:(dc + 1) * CH1]
                eng.tensor_scalar(e2_sb[:, dc * CH1:(dc + 1) * CH1], sl,
                                  vecs[:, 1, dc:dc + 1], 0.0, op0=AL.subtract, op1=AL.min)
                eng.tensor_single_scalar(e1_sb[:, dc * CH1:(dc + 1) * CH1], sl,
                                         vecs[:, 1, dc:dc + 1], op=AL.max)
            eng.drain()
            return eng.engine_nop()
        P.add("vector", sprep)

        def sexp(eng):
            for dc in range(DC):
                eng.activation(out=e1_sb[:, dc * CH1:(dc + 1) * CH1],
                               in_=e1_sb[:, dc * CH1:(dc + 1) * CH1], func=AF.Exp,
                               bias=negw[:, dc:dc + 1], scale=-1.0)
            ins = None
            for dc in range(DC):
                ins = eng.activation(out=e2_sb[:, dc * CH1:(dc + 1) * CH1],
                                     in_=e2_sb[:, dc * CH1:(dc + 1) * CH1], func=AF.Exp)
            return ins
        P.add("scalar", sexp)

        def scans(eng):
            eng.tensor_tensor(out=v_sb[:, :], in0=e2_sb[:, :], in1=v_sb[:, :], op=AL.mult)
            if first:
                eng.memset(st_a[:, :], 0.0)
                eng.memset(st_b[:, :], 0.0)
            eng.drain()
            for dc in range(DC):
                eng.tensor_tensor_scan(a_sb[:, dc * CH1:(dc + 1) * CH1],
                                       e1_sb[:, dc * CH1:(dc + 1) * CH1],
                                       v_sb[:, dc * CH1:(dc + 1) * CH1],
                                       st_a[:, dc:dc + 1], AL.mult, AL.add)
                eng.tensor_tensor_scan(b_sb[:, dc * CH1:(dc + 1) * CH1],
                                       e1_sb[:, dc * CH1:(dc + 1) * CH1],
                                       e2_sb[:, dc * CH1:(dc + 1) * CH1],
                                       st_b[:, dc:dc + 1], AL.mult, AL.add)
            eng.drain()
            eng.tensor_copy(out=st_a[:, :],
                            in_=a_sb[:, :].rearrange("p (c t) -> p c t", c=DC)[:, :, CH1 - 1])
            eng.tensor_copy(out=st_b[:, :],
                            in_=b_sb[:, :].rearrange("p (c t) -> p c t", c=DC)[:, :, CH1 - 1])
            eng.drain()
            eng.reciprocal(b_sb[:, :], b_sb[:, :])
            eng.drain()
            eng.tensor_tensor(out=a_sb[:, :], in0=a_sb[:, :], in1=b_sb[:, :], op=AL.mult)
            eng.drain()
            eng.tensor_tensor(out=a_sb[:, :], in0=a_sb[:, :], in1=r_sb[:, :], op=AL.mult)
            eng.drain()
            return eng.engine_nop()
        P.add("vector", scans)

        P.add("scalar", lambda e: e.copy(out=rwh_sb, in_=a_sb[:, :]))

        def rlo(eng):
            eng.tensor_tensor(out=rwl_sb, in0=a_sb[:, :], in1=rwh_sb, op=AL.subtract)
            eng.drain()
            return eng.engine_nop()
        P.add("vector", rlo)

        if b == 0 and ci == 0 and "dbg1" in debug:
            for nm, buf in (("dbg_xlnfm", xln_fm), ("dbg_k", k_sb), ("dbg_v", v_sb),
                            ("dbg_r", r_sb), ("dbg_e1", e1_sb), ("dbg_e2", e2_sb),
                            ("dbg_a", a_sb)):
                dd = nc.dram_tensor(nm, [128, buf.shape[1] if hasattr(buf, 'shape') else 0],
                                    F32, kind="ExternalOutput").ap()
                P.dma("scalar", lambda e, dd=dd, buf=buf: e.dma_start(out=dd[:, :], in_=buf[:, :]))
        col0 = b * T + t0
        P.dma("scalar", lambda e: e.dma_start(
            out=rwkv_hi_d.rearrange("(c p) t -> p c t", p=128)[:, :, col0:col0 + CH1],
            in_=rwh_sb.rearrange("p (c t) -> p c t", c=DC)))
        P.dma("scalar", lambda e: e.dma_start(
            out=rwkv_lo_d.rearrange("(c p) t -> p c t", p=128)[:, :, col0:col0 + CH1],
            in_=rwl_sb.rearrange("p (c t) -> p c t", c=DC)))

    if "p1a" in phases:
        for b in range(B):
            for ci in range(T // CH1):
                p1a_chunk(b, ci)

    es_a.close()

    es_b = ExitStack()
    wo_hi = sbt(es_b, "wo_hi", [128, DC * D], BF16)
    wo_lo = sbt(es_b, "wo_lo", [128, DC * D], BF16)
    stg_b = sbt(es_b, "stg_b", [128, D], F32)
    scr_b = sbt(es_b, "scr_b", [128, 512], F32)
    rwvh_sb = sbt(es_b, "rwvh_sb", [128, DC, CH2], BF16)
    rwvl_sb = sbt(es_b, "rwvl_sb", [128, DC, CH2], BF16)
    x4_sb = sbt(es_b, "x4_sb", [128, 4 * D], F32)
    ao_fm = sbt(es_b, "ao_fm", [128, DC * CH2], F32)
    xaf_hi = sbt(es_b, "xaf_hi", [128, DC * CH2], BF16)
    xaf_lo = sbt(es_b, "xaf_lo", [128, DC * CH2], BF16)
    xat_sb = sbt(es_b, "xat_sb", [128, 4 * D], F32)
    raw_sb = sbt(es_b, "raw_sb", [NE, CH2], F32)
    lgt_sb = sbt(es_b, "lgt_sb", [128, 4 * NE], F32)
    sc_sb = sbt(es_b, "sc_sb", [128, 4 * NE], F32)
    sq2_scr = sbt(es_b, "sq2_scr", [128, D], BF16)
    xl2 = sbt(es_b, "xl2", [128, D], F32)
    xl2h = sbt(es_b, "xl2h", [128, D], BF16)
    xl2l = sbt(es_b, "xl2l", [128, D], BF16)

    def p1b_chunk(b, ci):
        t0 = ci * CH2
        col0 = b * T + t0
        P.dma("sync", lambda e: e.dma_start(
            out=rwvh_sb[:, :, :],
            in_=rwkv_hi_d.rearrange("(c p) t -> p c t", p=128)[:, :, col0:col0 + CH2]))
        P.dma("sync", lambda e: e.dma_start(
            out=rwvl_sb[:, :, :],
            in_=rwkv_lo_d.rearrange("(c p) t -> p c t", p=128)[:, :, col0:col0 + CH2]))
        for ti in range(4):
            P.dma("sync", lambda e, ti=ti: e.dma_start(
                out=x4_sb[:, ti * D:(ti + 1) * D],
                in_=x_in[b, t0 + ti * 128:t0 + (ti + 1) * 128, :]))

        wopj = {}; woev = {}
        for oc in range(DC):
            def pj(eng, oc=oc):
                bank = psum[oc % 2]
                ins = None
                for dc in range(DC):
                    lh = wo_hi[:, dc * D + oc * 128:dc * D + oc * 128 + 128]
                    ll = wo_lo[:, dc * D + oc * 128:dc * D + oc * 128 + 128]
                    rh = rwvh_sb[:, dc, :]
                    rl = rwvl_sb[:, dc, :]
                    eng.matmul(bank[:, :], lh, rh, start=(dc == 0), stop=False)
                    eng.matmul(bank[:, :], lh, rl, start=False, stop=False)
                    ins = eng.matmul(bank[:, :], ll, rh, start=False, stop=(dc == DC - 1))
                return ins
            if oc == 0:
                wopj[oc] = P.add("tensor", pj)
            else:
                wopj[oc] = P.addm("tensor", pj, [("scalar", woev.get(oc - 2))])
            woev[oc] = P.addm("scalar", lambda e, oc=oc: e.activation(
                out=ao_fm[:, oc * CH2:(oc + 1) * CH2], in_=psum[oc % 2][:, :],
                func=AF.Identity, bias=vecs[:, 5, oc:oc + 1]), [("tensor", wopj[oc])])

        for ti in range(4):
            for g in range(2):
                def tr(eng, ti=ti, g=g):
                    ins = None
                    for j in range(4):
                        dc = g * 4 + j
                        ins = eng.transpose(psum[4 + j][:, 0:128],
                                            x4_sb[:, ti * D + dc * 128:ti * D + (dc + 1) * 128],
                                            ident_f[:, :])
                    return ins
                P.add("tensor", tr)

                def ev(eng, ti=ti, g=g):
                    for j in range(4):
                        dc = g * 4 + j
                        o = dc * CH2 + ti * 128
                        eng.scalar_tensor_tensor(ao_fm[:, o:o + 128], psum[4 + j][:, 0:128],
                                                 1.0, ao_fm[:, o:o + 128],
                                                 op0=AL.mult, op1=AL.add)
                    eng.drain()
                    return eng.engine_nop()
                P.add("vector", ev)

        P.add("scalar", lambda e: e.copy(out=xaf_hi[:, :], in_=ao_fm[:, :]))

        def xlo(eng):
            eng.tensor_tensor(out=xaf_lo[:, :], in0=ao_fm[:, :], in1=xaf_hi[:, :], op=AL.subtract)
            eng.drain()
            return eng.engine_nop()
        P.add("vector", xlo)

        def rawmm(eng):
            bank = psum[2]
            ins = None
            for dc in range(DC):
                eng.matmul(bank[0:NE, 0:CH2], wrh[:, dc, :], xaf_hi[:, dc * CH2:(dc + 1) * CH2],
                           start=(dc == 0), stop=False)
                eng.matmul(bank[0:NE, 0:CH2], wrh[:, dc, :], xaf_lo[:, dc * CH2:(dc + 1) * CH2],
                           start=False, stop=False)
                ins = eng.matmul(bank[0:NE, 0:CH2], wrl[:, dc, :],
                                 xaf_hi[:, dc * CH2:(dc + 1) * CH2],
                                 start=False, stop=(dc == DC - 1))
            return ins
        P.add("tensor", rawmm)
        P.add("vector", lambda e: e.tensor_copy(out=raw_sb[:, :], in_=psum[2][0:NE, 0:CH2]))

        for ti in range(4):
            for g in range(2):
                def tr2(eng, ti=ti, g=g):
                    ins = None
                    for j in range(4):
                        dc = g * 4 + j
                        o = dc * CH2 + ti * 128
                        ins = eng.transpose(psum[4 + j][:, 0:128], ao_fm[:, o:o + 128],
                                            ident_f[:, :])
                    return ins
                P.add("tensor", tr2)

                def ev2(eng, ti=ti, g=g):
                    for j in range(4):
                        dc = g * 4 + j
                        eng.tensor_copy(out=xat_sb[:, ti * D + dc * 128:ti * D + (dc + 1) * 128],
                                        in_=psum[4 + j][:, 0:128])
                    eng.drain()
                    return eng.engine_nop()
                P.add("vector", ev2)

        def trraw(eng):
            ins = None
            for ti in range(4):
                ins = eng.transpose(psum[4 + ti][:, 0:NE], raw_sb[:, ti * 128:(ti + 1) * 128],
                                    ident_f[0:NE, 0:NE])
            return ins
        P.add("tensor", trraw)

        def evraw(eng):
            for ti in range(4):
                eng.tensor_copy(out=lgt_sb[:, ti * NE:(ti + 1) * NE], in_=psum[4 + ti][:, 0:NE])
            eng.drain()
            return eng.engine_nop()
        P.add("vector", evraw)

        for ti in range(4):
            trow = col0 + ti * 128
            tile_idx = b * (T // 128) + ci * 4 + ti
            xt = xat_sb[:, ti * D:(ti + 1) * D]

            def ln2a(eng, xt=xt):
                eng.reduce_sum(stat[:, 0:1], xt, axis=AX)
                eng.drain()
                return eng.engine_nop()
            P.add("vector", ln2a)
            P.add("scalar", lambda e, xt=xt: e.activation(out=sq2_scr[:, :], in_=xt,
                                                          func=AF.Square, accum_out=stat[:, 2:3]))

            def ln2b(eng):
                eng.tensor_scalar_mul(stat[:, 1:2], stat[:, 0:1], 1.0 / D)
                eng.tensor_scalar_mul(stat[:, 3:4], stat[:, 2:3], 1.0 / D)
                eng.drain()
                eng.tensor_tensor(stat[:, 4:5], stat[:, 1:2], stat[:, 1:2], op=AL.mult)
                eng.drain()
                eng.tensor_tensor(stat[:, 5:6], stat[:, 3:4], stat[:, 4:5], op=AL.subtract)
                eng.drain()
                eng.tensor_scalar_add(stat[:, 5:6], stat[:, 5:6], 1e-5)
                eng.drain()
                return eng.engine_nop()
            P.add("vector", ln2b)
            P.add("scalar", lambda e: e.activation(out=stat[:, 6:7], in_=stat[:, 5:6], func=AF.Sqrt))

            def ln2c(eng, xt=xt):
                eng.reciprocal(stat[:, 7:8], stat[:, 6:7])
                eng.drain()
                eng.tensor_scalar(xl2[:, :], xt, stat[:, 1:2], stat[:, 7:8],
                                  op0=AL.subtract, op1=AL.mult)
                eng.drain()
                eng.tensor_tensor(out=xt, in0=xt, in1=bias_bc[:, :], op=AL.add)
                eng.drain()
                return eng.engine_nop()
            P.add("vector", ln2c)
            P.add("scalar", lambda e: e.copy(out=xl2h[:, :], in_=xl2[:, :]))

            def xl2lo(eng):
                eng.tensor_tensor(out=xl2l[:, :], in0=xl2[:, :], in1=xl2h[:, :], op=AL.subtract)
                eng.drain()
                return eng.engine_nop()
            P.add("vector", xl2lo)
            P.dma("scalar", lambda e, trow=trow: e.dma_start(
                out=xln2_cat_d[trow:trow + 128, 0:D], in_=xl2h[:, :]))
            P.dma("scalar", lambda e, trow=trow: e.dma_start(
                out=xln2_cat_d[trow:trow + 128, D:2 * D], in_=xl2l[:, :]))
            P.dma("scalar", lambda e, trow=trow, xt=xt: e.dma_start(out=out[trow:trow + 128, :],
                                                                    in_=xt))

            def smax(eng, ti=ti):
                sc = sc_sb[:, ti * NE:(ti + 1) * NE]
                eng.scalar_tensor_tensor(sc, cs_bc[:, :], stat[:, 1:2],
                                         lgt_sb[:, ti * NE:(ti + 1) * NE],
                                         op0=AL.mult, op1=AL.subtract)
                eng.tensor_scalar_mul(stat[:, 8:9], stat[:, 7:8], -1.0)
                eng.drain()
                eng.tensor_scalar_mul(sc, sc, stat[:, 8:9])
                eng.drain()
                eng.reduce_max(wrk[:, 0:1], sc, axis=AX)
                eng.drain()
                eng.tensor_scalar_mul(wrk[:, 1:2], wrk[:, 0:1], -1.0)
                eng.drain()
                return eng.engine_nop()
            P.add("vector", smax)
            P.add("scalar", lambda e, ti=ti: e.activation(
                out=sc_sb[:, ti * NE:(ti + 1) * NE], in_=sc_sb[:, ti * NE:(ti + 1) * NE],
                func=AF.Exp, bias=wrk[:, 1:2]))

            def smax2(eng, ti=ti, tile_idx=tile_idx):
                sc = sc_sb[:, ti * NE:(ti + 1) * NE]
                m = meta_sb[:, tile_idx, :]
                eng.reduce_sum(wrk[:, 2:3], sc, axis=AX)
                eng.drain()
                eng.reciprocal(wrk[:, 3:4], wrk[:, 2:3])
                eng.drain()
                eng.tensor_scalar_mul(sc, sc, wrk[:, 3:4])
                eng.drain()
                eng.reduce_max(m[:, 0:1], sc, axis=AX)
                eng.drain()
                eng.tensor_scalar(wrk[:, 4:12], sc, m[:, 0:1], None, op0=AL.is_equal)
                eng.drain()
                eng.scalar_tensor_tensor(wrk[:, 12:20], wrk[:, 4:12], -1e30, sc,
                                         op0=AL.mult, op1=AL.add)
                eng.drain()
                eng.reduce_max(m[:, 1:2], wrk[:, 12:20], axis=AX)
                eng.drain()
                eng.tensor_scalar(wrk[:, 12:20], wrk[:, 12:20], m[:, 1:2], None, op0=AL.is_equal)
                eng.drain()
                eng.tensor_tensor(wrk[:, 20:28], wrk[:, 4:12], iota8[:, :], op=AL.mult)
                eng.drain()
                eng.reduce_sum(m[:, 2:3], wrk[:, 20:28], axis=AX)
                eng.drain()
                eng.tensor_tensor(wrk[:, 20:28], wrk[:, 12:20], iota8[:, :], op=AL.mult)
                eng.drain()
                eng.reduce_sum(m[:, 3:4], wrk[:, 20:28], axis=AX)
                eng.drain()
                eng.tensor_tensor(wrk[:, 4:12], wrk[:, 4:12], wrk[:, 12:20], op=AL.add)
                eng.drain()
                return eng.engine_nop()
            P.add("vector", smax2)

            if ci == 0 and ti == 0:
                def czero(eng):
                    eng.memset(carry[:, :], 0.0)
                    eng.drain()
                    return eng.engine_nop()
                P.add("vector", czero)

            def pref(eng):
                eng.matmul(psum[3][:, 0:NE], Lstr_f[:, :], wrk[:, 4:12], start=True, stop=False)
                eng.matmul(psum[3][:, 0:NE], ones128[0:1, :], carry[0:1, :],
                           start=False, stop=True)
                eng.matmul(psum[2][0:1, 0:NE], ones128[:, 0:1], wrk[:, 4:12],
                           start=True, stop=False)
                return eng.matmul(psum[2][0:1, 0:NE], ones128[0:1, 0:1], carry[0:1, :],
                                  start=False, stop=True)
            P.add("tensor", pref)

            def pref_ev(eng, tile_idx=tile_idx):
                m = meta_sb[:, tile_idx, :]
                eng.tensor_copy(out=wrk[:, 12:20], in_=psum[3][:, 0:NE])
                eng.tensor_copy(out=carry[0:1, :], in_=psum[2][0:1, 0:NE])
                eng.drain()
                eng.tensor_scalar(wrk[:, 20:28], iota8[:, :], m[:, 2:3], None, op0=AL.is_equal)
                eng.drain()
                eng.tensor_tensor(wrk[:, 20:28], wrk[:, 20:28], wrk[:, 12:20], op=AL.mult)
                eng.drain()
                eng.reduce_sum(m[:, 4:5], wrk[:, 20:28], axis=AX)
                eng.drain()
                eng.tensor_scalar(wrk[:, 20:28], iota8[:, :], m[:, 3:4], None, op0=AL.is_equal)
                eng.drain()
                eng.tensor_tensor(wrk[:, 20:28], wrk[:, 20:28], wrk[:, 12:20], op=AL.mult)
                eng.drain()
                eng.reduce_sum(m[:, 5:6], wrk[:, 20:28], axis=AX)
                eng.drain()
                return eng.engine_nop()
            P.add("vector", pref_ev)


        if ci == T // CH2 - 1:
            P.dma("scalar", lambda e, b=b: e.dma_start(out=cc_in_d[b:b + 1, :], in_=carry[0:1, :]))

    if "p1b" in phases:
        prep_attnw(Wo_in, wo_hi, wo_lo, x4_sb, scr_b)
        for b in range(B):
            for ci in range(T // CH2):
                p1b_chunk(b, ci)

        def zrow(eng):
            eng.memset(xl2h[:, :], 0.0)
            eng.memset(xl2l[:, :], 0.0)
            eng.drain()
            return eng.engine_nop()
        P.add("vector", zrow)
        P.dma("scalar", lambda e: e.dma_start(out=xln2_cat_d[DUMMY:DUMMY + 1, 0:D], in_=xl2h[0:1, :]))
        P.dma("scalar", lambda e: e.dma_start(out=xln2_cat_d[DUMMY:DUMMY + 1, D:2 * D], in_=xl2l[0:1, :]))

    def p15():
        def tinit(eng):
            eng.memset(x4_sb[:, 0:161], 0.0)
            eng.drain()
            return eng.engine_nop()
        P.add("vector", tinit)
        P.dma("scalar", lambda e: e.dma_start(
            out=tokplane_d[0:NSLOT_PAD].rearrange("(p a) -> p a", p=128),
            in_=x4_sb[:, 0:161]))
        P.dma("scalar", lambda e: e.dma_start(
            out=gateplane_d[0:NSLOT_PAD].rearrange("(p a) -> p a", p=128),
            in_=x4_sb[:, 0:161]))

        def ag(eng):
            return eng.collective_compute(
                "AllGather", AL.bypass, replica_groups=[list(range(8))],
                ins=[cc_in_d[:, :]], outs=[cc_out_d[:, :]])
        P.add("gpsimd", ag)
        P.dma("sync", lambda e: e.dma_start(out=x4_sb[0:32, 1024:1024 + NE], in_=cc_out_d[:, :]))
        if cc_dbg_d is not None:
            P.dma("gpsimd", lambda e: e.dma_start(out=cc_dbg_d[:, :], in_=x4_sb[0:32, 1024:1024 + NE]))
        P.dma("sync", lambda e: e.dma_start(out=x4_sb[0:32, 1040:1045], in_=maskT_in[:, :]))

        P.add("tensor", lambda e: e.matmul(psum[3][0:5, 0:NE], x4_sb[0:32, 1040:1045],
                                           x4_sb[0:32, 1024:1024 + NE], start=True, stop=True))

        def basev(eng):
            eng.tensor_copy(out=x4_sb[0:5, 2048:2048 + NE], in_=psum[3][0:5, 0:NE])
            eng.drain()
            return eng.engine_nop()
        P.add("vector", basev)
        P.dma("gpsimd", lambda e: e.dma_start(out=x4_sb[0:1, 2560:2560 + 40],
                                              in_=x4_sb[0:5, 2048:2048 + NE]))

        for b_ in range(B):
            P.add("tensor", lambda e, b_=b_: e.matmul(
                psum[3][:, 0:NE], ones128[0:1, :], x4_sb[0:1, 2560 + b_ * NE:2560 + (b_ + 1) * NE],
                start=True, stop=True))
            P.add("vector", lambda e, b_=b_: e.tensor_copy(out=lcb_bc[:, b_, :],
                                                           in_=psum[3][:, 0:NE]))
        P.add("tensor", lambda e: e.matmul(psum[3][:, 0:NE], ones128[0:1, :],
                                           x4_sb[0:1, 2592:2600], start=True, stop=True))

        def crossv(eng):
            eng.tensor_copy(out=wrk[:, 0:8], in_=psum[3][:, 0:NE])
            eng.drain()
            for b_ in range(B):
                eng.tensor_tensor(lcb_bc[:, b_, :], lcb_bc[:, b_, :], wrk[:, 0:8],
                                  op=AL.subtract)
            eng.tensor_scalar(tau_bc[:, :], wrk[:, 0:8], -1.0, float(CAP),
                              op0=AL.mult, op1=AL.add)
            eng.drain()
            return eng.engine_nop()
        P.add("vector", crossv)

        for tile_idx in range(64):
            b_ = tile_idx // 16
            trow = tile_idx * 128

            def slots(eng, tile_idx=tile_idx, b_=b_, trow=trow):
                m = meta_sb[:, tile_idx, :]
                for ch, (ecol, rcol) in enumerate(((2, 4), (3, 5))):
                    w0 = 4 + ch * 12
                    oh = wrk[:, w0:w0 + 8]
                    eng.tensor_scalar(oh, iota8[:, :], m[:, ecol:ecol + 1], None, op0=AL.is_equal)
                    eng.drain()
                    eng.tensor_tensor(oh, oh, lcb_bc[:, b_, :], op=AL.mult)
                    eng.drain()
                    eng.reduce_sum(wrk[:, w0 + 8:w0 + 9], oh, axis=AX)
                    eng.drain()
                    eng.tensor_tensor(wrk[:, w0 + 8:w0 + 9], wrk[:, w0 + 8:w0 + 9],
                                      m[:, rcol:rcol + 1], op=AL.add)
                    eng.drain()
                    eng.tensor_scalar(wrk[:, w0 + 9:w0 + 10], m[:, ecol:ecol + 1],
                                      float(C_LOC), None, op0=AL.mult)
                    eng.drain()
                    eng.tensor_tensor(wrk[:, w0 + 9:w0 + 10], wrk[:, w0 + 9:w0 + 10],
                                      wrk[:, w0 + 8:w0 + 9], op=AL.add)
                    eng.drain()
                    eng.tensor_scalar(wrk[:, w0 + 10:w0 + 11], wrk[:, w0 + 8:w0 + 9],
                                      float(C_LOC), None, op0=AL.is_ge)
                    eng.drain()
                    eng.tensor_scalar(wrk[:, w0 + 11:w0 + 12], wrk[:, w0 + 9:w0 + 10],
                                      -1.0, float(NSLOT_PAD), op0=AL.mult, op1=AL.add)
                    eng.drain()
                    eng.tensor_tensor(wrk[:, w0 + 11:w0 + 12], wrk[:, w0 + 11:w0 + 12],
                                      wrk[:, w0 + 10:w0 + 11], op=AL.mult)
                    eng.drain()
                    eng.tensor_tensor(wrk[:, w0 + 9:w0 + 10], wrk[:, w0 + 9:w0 + 10],
                                      wrk[:, w0 + 11:w0 + 12], op=AL.add)
                    eng.drain()
                eng.tensor_copy(out=wrk_i[:, 0:1], in_=wrk[:, 13:14])
                eng.tensor_copy(out=wrk_i[:, 1:2], in_=wrk[:, 25:26])
                eng.tensor_scalar_add(wrk[:, 0:1], iotap[:, :], float(trow))
                eng.drain()
                return eng.engine_nop()
            P.add("vector", slots)
            for chn in range(2):
                def scat(eng, chn=chn):
                    return eng.indirect_dma_start(
                        out=tokplane_d.rearrange("(a o) -> a o", o=1),
                        out_offset=IndirectOffsetOnAxis(wrk_i[:, chn:chn + 1], 0),
                        in_=wrk[:, 0:1], in_offset=None)
                P.dma("gpsimd", scat)

                def scatg(eng, chn=chn, tile_idx=tile_idx):
                    return eng.indirect_dma_start(
                        out=gateplane_d.rearrange("(a o) -> a o", o=1),
                        out_offset=IndirectOffsetOnAxis(wrk_i[:, chn:chn + 1], 0),
                        in_=meta_sb[:, tile_idx, chn:chn + 1], in_offset=None)
                P.dma("gpsimd", scatg)

        def idxz(eng):
            eng.memset(idx16[:, :], 0)
            eng.drain()
            return eng.engine_nop()
        P.add("vector", idxz)
        for t_ in range(11):
            r0 = t_ * 128
            rows = min(128, 1280 - r0)
            if rows <= 0:
                continue
            P.dma("sync", lambda e, r0=r0, rows=rows: e.dma_start(
                out=x4_sb[0:rows, 0:16],
                in_=tokplane_d[r0 * 16:(r0 + rows) * 16].rearrange("(a b) -> a b", b=16)))
            P.add("tensor", lambda e, rows=rows: e.transpose(
                psum[3][0:16, 0:rows], x4_sb[0:rows, 0:16], ident_f[0:rows, 0:rows]))

            def evt(eng, r0=r0, rows=rows):
                eng.tensor_copy(out=scr_b[0:16, 0:rows], in_=psum[3][0:16, 0:rows])
                eng.drain()
                eng.tensor_copy(out=idx16[0:16, r0:r0 + rows], in_=scr_b[0:16, 0:rows])
                eng.drain()
                return eng.engine_nop()
            P.add("vector", evt)

        P.dma("gpsimd", lambda e: e.dma_start(out=idx16[16:32, :], in_=idx16[0:16, :]))

    if "p15" in phases:
        p15()

    es_b.close()

    es_c = ExitStack()
    NTK = C_LOC // 256
    FCQ = 8
    w1c_hi = sbt(es_c, "w1c_hi", [128, FCQ * D], BF16)
    w1c_lo = sbt(es_c, "w1c_lo", [128, FCQ * D], BF16)
    w2c_hi = sbt(es_c, "w2c_hi", [128, FCQ * D], BF16)
    w2c_lo = sbt(es_c, "w2c_lo", [128, FCQ * D], BF16)
    wstg = sbt(es_c, "wstg", [128, 4, D], F32)
    xg = [sbt(es_c, f"xg{j}", [128, 2 * DC, 512], BF16) for j in range(5)]

    hf_sb = sbt(es_c, "hf_sb", [128, 2, 256], F32)
    hh_sb = sbt(es_c, "hh_sb", [128, 2, 256], BF16)
    hl_sb = sbt(es_c, "hl_sb", [128, 2, 256], BF16)
    g_sb = sbt(es_c, "g_sb", [128, NTK * 2], F32)
    tk_i = sbt(es_c, "tk_i", [128, NTK * 2], I32)
    y_sb = sbt(es_c, "y_sb", [128, 2, D], F32)
    idxw = sbt(es_c, "idxw", [128, C_LOC // 16], I16)
    idxg = sbt(es_c, "idxg", [128, 32], I16)
    pl_sb = sbt(es_c, "pl_sb", [128, 128], F32)

    def p2_expert(e_, p2_state):
        c0 = (e_ * C_LOC) // 16
        PP = P2PARTS

        def idxcp(eng):
            eng.tensor_copy(out=idxw[:, :], in_=idx16[:, c0:c0 + C_LOC // 16])
            eng.drain()
            return eng.engine_nop()
        P.add("vector", idxcp)
        if "g" not in PP:
            return
        for j in range(5):
            def idxcp2(eng, j=j):
                eng.tensor_copy(out=idxg[:, :], in_=idxw[:, j * 32:(j + 1) * 32])
                eng.drain()
                return eng.engine_nop()
            P.add("vector", idxcp2)
            P.dma("gpsimd", lambda e, j=j: e.dma_gather(
                xg[j][:, :, :], xln2_cat_d[:, :], idxg[:, :],
                num_idxs=512, num_idxs_reg=512, elem_size=2 * D, transpose=True,
                single_packet=False))

        if "c" not in PP:
            return

        P.dma("sync", lambda e, e_=e_: e.dma_start(
            out=pl_sb[0:20, 0:128],
            in_=tokplane_d[e_ * C_LOC:(e_ + 1) * C_LOC].rearrange("(a b) -> a b", b=128)))
        P.dma("sync", lambda e, e_=e_: e.dma_start(
            out=pl_sb[20:40, 0:128],
            in_=gateplane_d[e_ * C_LOC:(e_ + 1) * C_LOC].rearrange("(a b) -> a b", b=128)))
        P.add("tensor", lambda e: e.transpose(psum[3][:, 0:40], pl_sb[0:40, 0:128],
                                              ident_f[0:40, 0:40]))

        def gates(eng, e_=e_):
            eng.tensor_copy(out=wrk[:, 0:20], in_=psum[3][:, 0:20])
            eng.drain()
            eng.tensor_copy(out=tk_i[:, :], in_=wrk[:, 0:20])
            eng.tensor_copy(out=g_sb[:, :], in_=psum[3][:, 20:40])
            eng.drain()
            for sub in range(NTK * 2):
                eng.tensor_scalar_add(wrk[:, 20:21], iotap[:, :], float(sub * 128))
                eng.drain()
                eng.tensor_tensor(wrk[:, 20:21], wrk[:, 20:21], tau_bc[:, e_:e_ + 1],
                                  op=AL.is_lt)
                eng.drain()
                eng.tensor_tensor(g_sb[:, sub:sub + 1], g_sb[:, sub:sub + 1],
                                  wrk[:, 20:21], op=AL.mult)
                eng.drain()
            return eng.engine_nop()
        P.add("vector", gates)

        if "w" not in PP:
            return
        st = p2_state
        for q in range(4):
            qbar = P.add("tensor", lambda e: e.matmul(psum[3][0:1, 0:1], ones128[0:1, 0:1], ones128[0:1, 0:1], start=True, stop=True))
            w1hi_v = {}; w1lo_v = {}; w2hi_v = {}; w2lo_v = {}
            ld1_v = {}; ld2_v = {}
            for fc in range(FCQ):
                fa = q * (FFN // 4) + fc * 128
                sg1 = fc % 2
                sg2 = 2 + fc % 2
                _ldfn1 = lambda e, e_=e_, fa=fa, sg1=sg1: e.dma_start(
                    out=wstg[:, sg1, :].rearrange("p (c f) -> p c f", c=DC),
                    in_=w1_in[e_].rearrange("(c p) f -> p c f", p=128)[:, :, fa:fa + 128])
                if "c" in P2RELAX:
                    ld1_v[fc] = P.dmam("sync", _ldfn1,
                        [("tensor", qbar), ("scalar", w1hi_v.get(fc - 2)),
                         ("vector", w1lo_v.get(fc - 2))], ctr=f"ldw1{fc % 2}")
                else:
                    ld1_v[fc] = P.dma("sync", _ldfn1)
                _hifn1 = lambda e, fc=fc, sg1=sg1: e.copy(
                    out=w1c_hi[:, fc * D:(fc + 1) * D], in_=wstg[:, sg1, :])
                if "c" in P2RELAX:
                    w1hi_v[fc] = P.addm("scalar", _hifn1, [(f"ldw1{fc % 2}", ld1_v[fc])])
                else:
                    w1hi_v[fc] = P.add("scalar", _hifn1)

                def wlo1(eng, fc=fc, sg1=sg1):
                    eng.tensor_tensor(out=w1c_lo[:, fc * D:(fc + 1) * D], in0=wstg[:, sg1, :],
                                      in1=w1c_hi[:, fc * D:(fc + 1) * D], op=AL.subtract)
                    eng.drain()
                    return eng.engine_nop()
                if "c" in P2RELAX:
                    w1lo_v[fc] = P.addm("vector", wlo1, [("scalar", w1hi_v[fc])])
                else:
                    w1lo_v[fc] = P.add("vector", wlo1)

                _ldfn2 = lambda e, e_=e_, fa=fa, sg2=sg2: e.dma_start(
                    out=wstg[:, sg2, :], in_=w2_in[e_, fa:fa + 128, :])
                if "c" in P2RELAX:
                    ld2_v[fc] = P.dmam("sync", _ldfn2,
                        [("tensor", qbar), ("scalar", w2hi_v.get(fc - 2)),
                         ("vector", w2lo_v.get(fc - 2))], ctr=f"ldw2{fc % 2}")
                else:
                    ld2_v[fc] = P.dma("sync", _ldfn2)
                _hifn2 = lambda e, fc=fc, sg2=sg2: e.copy(
                    out=w2c_hi[:, fc * D:(fc + 1) * D], in_=wstg[:, sg2, :])
                if "c" in P2RELAX:
                    w2hi_v[fc] = P.addm("scalar", _hifn2, [(f"ldw2{fc % 2}", ld2_v[fc])])
                else:
                    w2hi_v[fc] = P.add("scalar", _hifn2)

                def wlo2(eng, fc=fc, sg2=sg2):
                    eng.tensor_tensor(out=w2c_lo[:, fc * D:(fc + 1) * D], in0=wstg[:, sg2, :],
                                      in1=w2c_hi[:, fc * D:(fc + 1) * D], op=AL.subtract)
                    eng.drain()
                    return eng.engine_nop()
                if "c" in P2RELAX:
                    w2lo_v[fc] = P.addm("vector", wlo2, [("scalar", w2hi_v[fc])])
                else:
                    w2lo_v[fc] = P.add("vector", wlo2)

            if "h" not in PP:
                continue
            for k in range(NTK):
                def mk_hmm(fc, k):
                    def hmm(eng):
                        bank = psum[fc % 2]
                        ins = None
                        for dc in range(DC):
                            lh = w1c_hi[:, fc * D + dc * 128:fc * D + (dc + 1) * 128]
                            ll = w1c_lo[:, fc * D + dc * 128:fc * D + (dc + 1) * 128]
                            rh = xg[k // 2][:, dc, (k % 2) * 256:(k % 2) * 256 + 256]
                            rl = xg[k // 2][:, DC + dc, (k % 2) * 256:(k % 2) * 256 + 256]
                            eng.matmul(bank[:, 0:256], lh, rh, start=(dc == 0), stop=False)
                            eng.matmul(bank[:, 0:256], lh, rl, start=False, stop=False)
                            ins = eng.matmul(bank[:, 0:256], ll, rh, start=False,
                                             stop=(dc == DC - 1))
                        return ins
                    return hmm

                def mk_gelu(fc, k):
                    def gelu(eng):
                        b = fc % 2
                        eng.activation(out=hf_sb[:, b, :], in_=psum[fc % 2][:, 0:256],
                                       func=AF.Gelu_apprx_tanh)
                        eng.drain()
                        return eng.copy(out=hh_sb[:, b, :], in_=hf_sb[:, b, :])
                    return gelu

                def mk_losub(fc, k):
                    def losub(eng):
                        b = fc % 2
                        eng.tensor_tensor(out=hl_sb[:, b, :], in0=hf_sb[:, b, :],
                                          in1=hh_sb[:, b, :], op=AL.subtract)
                        eng.drain()
                        return eng.engine_nop()
                    return losub

                def mk_ymm(fc, k):
                    def ymm(eng):
                        b = fc % 2
                        ins = None
                        for sub in range(2):
                            for dh in range(2):
                                bank = psum[4 + sub * 2 + dh]
                                lh = hh_sb[:, b, sub * 128:(sub + 1) * 128]
                                ll = hl_sb[:, b, sub * 128:(sub + 1) * 128]
                                rh = w2c_hi[:, fc * D + dh * 512:fc * D + (dh + 1) * 512]
                                rl = w2c_lo[:, fc * D + dh * 512:fc * D + (dh + 1) * 512]
                                eng.matmul(bank[:, :], lh, rh, start=(fc == 0), stop=False)
                                eng.matmul(bank[:, :], lh, rl, start=False, stop=False)
                                ins = eng.matmul(bank[:, :], ll, rh, start=False,
                                                 stop=(fc == FCQ - 1))
                        return ins
                    return ymm

                for fc in range(FCQ):
                    if "h" in P2RELAX:
                        st["hmm"][(k, fc)] = P.addm("tensor", mk_hmm(fc, k),
                            [("scalar", w1hi_v[fc]), ("vector", w1lo_v[fc]),
                             ("scalar", w2hi_v[fc]), ("vector", w2lo_v[fc]),
                             ("scalar", st["gelu"].get((k, fc - 2)) or st["gelu"].get((k - 1, FCQ + fc - 2)))])
                    else:
                        st["hmm"][(k, fc)] = P.add("tensor", mk_hmm(fc, k))
                    if fc >= 1 and "y" in PP:
                        if "y" in P2RELAX:
                            st["ymm"][(k, fc - 1)] = P.addm("tensor", mk_ymm(fc - 1, k),
                                [("vector", st["losub"][(k, fc - 1)]),
                                 ("vector", st["yev"].get(k - 1) if fc - 1 == 0 else None)])
                        else:
                            st["ymm"][(k, fc - 1)] = P.add("tensor", mk_ymm(fc - 1, k))
                    if "g" in P2RELAX:
                        st["gelu"][(k, fc)] = P.addm("scalar", mk_gelu(fc, k),
                            [("tensor", st["hmm"][(k, fc)]),
                             ("vector", st["losub"].get((k, fc - 2)) or st["losub"].get((k - 1, FCQ + fc - 2))),
                             ("tensor", st["ymm"].get((k, fc - 2)) or st["ymm"].get((k - 1, FCQ + fc - 2)))])
                        st["losub"][(k, fc)] = P.addm("vector", mk_losub(fc, k),
                            [("scalar", st["gelu"][(k, fc)]),
                             ("tensor", st["ymm"].get((k, fc - 2)) or st["ymm"].get((k - 1, FCQ + fc - 2)))])
                    else:
                        st["gelu"][(k, fc)] = P.add("scalar", mk_gelu(fc, k))
                        st["losub"][(k, fc)] = P.add("vector", mk_losub(fc, k))
                if "y" not in PP:
                    continue
                if "y" in P2RELAX:
                    st["ymm"][(k, FCQ - 1)] = P.addm("tensor", mk_ymm(FCQ - 1, k),
                        [("vector", st["losub"][(k, FCQ - 1)])])
                else:
                    st["ymm"][(k, FCQ - 1)] = P.add("tensor", mk_ymm(FCQ - 1, k))

                for sub in range(2):
                    def yev(eng, sub=sub, k=k):
                        for dh in range(2):
                            eng.tensor_scalar_mul(y_sb[:, sub, dh * 512:(dh + 1) * 512],
                                                  psum[4 + sub * 2 + dh][:, :],
                                                  g_sb[:, 2 * k + sub:2 * k + sub + 1])
                        eng.drain()
                        return eng.engine_nop()
                    if "y" in P2RELAX:
                        yv = P.addm("vector", yev,
                                    [("tensor", st["ymm"][(k, FCQ - 1)]),
                                     ("gpsimd", st["yscat"].get((k - 1, sub)))])
                    else:
                        yv = P.add("vector", yev)
                    if sub == 1:
                        st["yev"][k] = yv
                    if "s" not in PP:
                        continue

                    def yscat(eng, sub=sub, k=k):
                        return eng.indirect_dma_start(
                            out=out[:, :],
                            out_offset=IndirectOffsetOnAxis(tk_i[:, 2 * k + sub:2 * k + sub + 1], 0),
                            in_=y_sb[:, sub, :], in_offset=None, compute_op=AL.add)
                    if "y" in P2RELAX:
                        st["yscat"][(k, sub)] = P.dmam("gpsimd", yscat,
                            [("vector", yv), ("gpsimd", st["lastscat"][0])])
                    else:
                        st["yscat"][(k, sub)] = P.dma("gpsimd", yscat)
                    st["lastscat"][0] = st["yscat"][(k, sub)]

    if "p2" in phases:
        P.add("gpsimd", lambda e: (e.load_library(LIB_MLP), e.engine_nop())[-1])
        for nm in ("ldw10", "ldw11", "ldw20", "ldw21"):
            P.new_ctr(nm)
        p2_state = {"hmm": {}, "gelu": {}, "losub": {}, "ymm": {}, "yev": {},
                    "yscat": {}, "lastscat": [None]}
        for e_ in range(min(NE, P2N)):
            p2_expert(e_, p2_state)

    es_c.close()

    P.add("gpsimd", lambda e: e.engine_nop())

    block = es.enter_context(nc.Block())
    P.emit(block)
    es.close()
    mybir.codegen_inst_isa_subclasses(nc)
    return nc


def make_inputs(inputs, core):
    vv = np.stack([np.asarray(inputs[k], np.float32) for k in
                   ("L0_attn_u", "L0_attn_w", "L0_attn_tmk", "L0_attn_tmv", "L0_attn_tmr",
                    "L0_attn_bo")])
    mask = np.zeros((32, 5), np.float32)
    for bl in range(4):
        mask[: 4 * core + bl, bl] = 1.0
    mask[: 4 * core, 4] = 1.0
    return {
        "x": np.ascontiguousarray(np.asarray(inputs["L0_attn_x"], np.float32)[4 * core:4 * core + 4]),
        "Wk": np.asarray(inputs["L0_attn_Wk"], np.float32),
        "Wv": np.asarray(inputs["L0_attn_Wv"], np.float32),
        "Wr": np.asarray(inputs["L0_attn_Wr"], np.float32),
        "Wo": np.asarray(inputs["L0_attn_Wo"], np.float32),
        "vecs": vv,
        "w_router": np.asarray(inputs["L0_ffn_w_router"], np.float32),
        "w1": np.asarray(inputs["L0_ffn_w1"], np.float32),
        "w2": np.asarray(inputs["L0_ffn_w2"], np.float32),
        "bias": np.asarray(inputs["L0_ffn_bias"], np.float32).reshape(1, D),
        "maskT": mask,
    }


def kernel(**inputs):
    nc = build_program()
    in_maps = [make_inputs(inputs, core) for core in range(8)]
    res = run_bass_kernel_spmd(nc, in_maps, core_ids=list(range(8)))
    shards = [res.results[cu]["out"].reshape(B, T, D) for cu in range(8)]
    return np.concatenate(shards, axis=0)
